# version 02
# baseline (speedup 1.0000x reference)
import math
import contextlib
import numpy as np
import ml_dtypes

import concourse.bass as bass
import concourse.mybir as mybir
from concourse.bass_utils import run_bass_kernel_spmd

F32 = mybir.dt.float32
BF16 = mybir.dt.bfloat16
AF = mybir.ActivationFunctionType
ALU = mybir.AluOpType

S = 2048
E = 2048
H = 16
DH = 128
LOW = 512
R = 64
KCE = E // 128
SCALE_A = 1.0 / math.sqrt(DH + R)
SCALE_B = DH ** (-0.5)
NEG_EXP_BIAS = -12.0
EPS = 1e-5
NSEQ = 2

_TAPS = set()
_STOP_AFTER = None
import os as _os
_SAFE = bool(int(_os.environ.get("ATHENA_SAFE", "0")))
def _parse_mb(spec):
    out = set()
    for part in spec.split(","):
        part = part.strip()
        if not part:
            continue
        if "-" in part:
            a, b = part.split("-")
            out.update(range(int(a), int(b) + 1))
        else:
            out.add(int(part))
    return out


_MB_ON = _parse_mb(_os.environ.get("ATHENA_MB", ""))
_PB = {}
for _part in _os.environ.get("ATHENA_PB", "").split(","):
    if ":" in _part:
        _f = _part.split(":")
        _PB[int(_f[0])] = (_f[1].split("+"), _f[2].split("+") if len(_f) > 2 else None)


class _StopEmission(Exception):
    pass


BF = ml_dtypes.bfloat16



def lhsT_layout(W):
    K, O = W.shape
    Kc = K // 128
    ow = 128 if O >= 128 else O
    Oc = O // ow
    return np.ascontiguousarray(W.reshape(Kc, 128, Oc, ow).transpose(2, 1, 0, 3))


_ROPE_PERM = np.concatenate([np.arange(0, R, 2), np.arange(1, R, 2)])


def build_rope_tables():
    half = R // 2
    freqs = 10000.0 ** (-np.arange(half, dtype=np.float64) / half)
    theta = np.outer(freqs, np.arange(S, dtype=np.float64))
    cos = np.cos(theta).astype(np.float32)
    sin = np.sin(theta).astype(np.float32)
    csk = np.concatenate([cos, sin], 0)
    sck = np.concatenate([sin, cos], 0)
    r1 = np.zeros((64, 64), np.float32)
    r1[np.arange(32), np.arange(32)] = 1.0
    r1[np.arange(32) + 32, np.arange(32)] = -1.0
    r2 = np.zeros((64, 64), np.float32)
    r2[np.arange(32), np.arange(32) + 32] = 1.0
    r2[np.arange(32) + 32, np.arange(32) + 32] = 1.0
    rmix = np.concatenate([r1, r2], 1).astype(BF)
    return csk, sck, rmix


def build_maska():
    i = np.arange(128)[:, None]
    j = np.arange(512)[None, :]
    return np.ascontiguousarray(
        np.stack([(j >= d * 128 + i) for d in range(4)]).astype(BF))


def build_maskb():
    i = np.arange(128)[:, None]
    j = np.arange(128)[None, :]
    pats, masks, tiles = {}, [], {}
    for qb in range(16):
        lst = []
        for tk in range(16):
            k = tk * 128 + i
            q = qb * 128 + j
            v = (np.abs(k - q) < 256) | (k < 16) | (q < 16)
            if not v.any():
                continue
            if v.all():
                lst.append((tk, None))
            else:
                key = v.tobytes()
                if key not in pats:
                    pats[key] = len(masks)
                    masks.append(v.astype(BF))
                lst.append((tk, pats[key]))
        tiles[qb] = lst
    return tiles, np.ascontiguousarray(np.stack(masks))


MASKB_TILES, MASKB_ARR = build_maskb()
NMASKB = MASKB_ARR.shape[0]


def prepare_weights(inp):
    d = {}
    d["wdq"] = lhsT_layout(inp["L0_attn_a_W_DQ"])
    d["wdkv"] = lhsT_layout(inp["L0_attn_a_W_DKV"])
    d["wkr"] = lhsT_layout(np.ascontiguousarray(inp["L0_attn_a_W_KR"][:, _ROPE_PERM]))
    d["wuq"] = lhsT_layout(inp["L0_attn_a_W_UQ"])
    wqr = inp["L0_attn_a_W_QR"].reshape(E, H, R)[:, :, _ROPE_PERM]
    d["wqr2"] = np.ascontiguousarray(
        wqr.reshape(KCE, 128, 8, 128).transpose(2, 1, 0, 3))
    d["wuk"] = lhsT_layout(inp["L0_attn_a_W_UK"])
    d["wuvr"] = np.ascontiguousarray(
        inp["L0_attn_a_W_UV"].reshape(4, 128, H, 128).transpose(2, 0, 1, 3))
    d["wo"] = lhsT_layout(inp["L0_attn_a_W_O"])
    d["wq"] = lhsT_layout(inp["L0_attn_b_Wq"])
    d["wk"] = lhsT_layout(inp["L0_attn_b_Wk"])
    d["wvr"] = np.ascontiguousarray(
        inp["L0_attn_b_Wv"].reshape(KCE, 128, 2, 1024).transpose(2, 0, 1, 3))
    d["woo"] = lhsT_layout(inp["L0_attn_b_Wo"])
    d["w1"] = lhsT_layout(np.ascontiguousarray(inp["L0_ffn_W1"].T))
    d["w2"] = lhsT_layout(np.ascontiguousarray(inp["L0_ffn_W2"].T))
    d["w3"] = lhsT_layout(np.ascontiguousarray(inp["L0_ffn_W3"].T))
    d["b1c"] = np.ascontiguousarray(inp["L0_ffn_b1"].reshape(32, 128).T).astype(np.float32)
    d["b2c"] = np.ascontiguousarray(inp["L0_ffn_b2"].reshape(32, 128).T).astype(np.float32)
    d["b3r"] = np.ascontiguousarray(inp["L0_ffn_b3"].reshape(1, E)).astype(np.float32)
    csk, sck, rmix = build_rope_tables()
    d["csk2"] = np.ascontiguousarray(np.concatenate([csk, csk], 0))
    d["sck2"] = np.ascontiguousarray(np.concatenate([sck, sck], 0))
    z = np.zeros((64, 64), np.float32)
    r1 = rmix[:, 0:64].astype(np.float32)
    r2 = rmix[:, 64:128].astype(np.float32)
    d["rmix2"] = np.ascontiguousarray(np.concatenate([
        np.block([[r1, z], [z, r1]]), np.block([[r2, z], [z, r2]])],
        1)).astype(BF)
    d["rmix"] = rmix
    d["maska"] = build_maska()
    d["maskb"] = MASKB_ARR
    return d



_WLOG = []
_ALLOCLOG = []


class P:
    def __init__(self, nc, block, sems):
        self.nc = nc
        self.block = block
        self.sems = sems
        self.c = {k: 0 for k in sems}
        self.hw = {}
        self.tap_names = []
        self.wsl_free = [None, None]
        self.bs_owner = [None, None]
        self.of32_free = [None, None]
        self.ln_prev = None

    def w(self, eng, ename, sem, val):
        if val is None or val <= 0:
            return
        key = (ename, sem)
        if self.hw.get(key, 0) >= val:
            _WLOG.append((ename, "wait-ELIDED", sem, val))
            return
        _WLOG.append((ename, "wait", sem, val))
        eng.wait_ge(self.sems[sem], val)
        self.hw[key] = val

    def wl(self, eng, ename, lst):
        if lst:
            for sem, val in lst:
                self.w(eng, ename, sem, val)

    def barrier(self):
        snap = dict(self.c)

        def mk(ename):
            def f(eng):
                for s, v in snap.items():
                    self.w(eng, ename, s, v)
            return f
        self.block.tensor(mk("pe"))
        self.block.vector(mk("dv"))
        self.block.scalar(mk("ac"))
        self.block.gpsimd(mk("gp"))
        self.block.sync(mk("sy"))
        self.wsl_free = [None, None]
        self.bs_owner = [None, None]
        self.of32_free = [None, None]
        self.ln_prev = None

    def mb(self):
        self._mb_i = getattr(self, "_mb_i", 0) + 1
        import inspect
        ln = inspect.currentframe().f_back.f_lineno
        if not hasattr(self, "mb_sites"):
            self.mb_sites = []
        self.mb_sites.append((self._mb_i, ln))
        if _SAFE or self._mb_i in _MB_ON:
            self.barrier()
        elif self._mb_i in _PB:
            snap = dict(self.c)
            regs = {"pe": self.block.tensor, "dv": self.block.vector,
                    "ac": self.block.scalar, "gp": self.block.gpsimd,
                    "sy": self.block.sync}

            engs, semsel = _PB[self._mb_i]

            def mkp(ename):
                def f(eng):
                    for s_, v in snap.items():
                        if semsel is None or s_ in semsel:
                            self.w(eng, ename, s_, v)
                return f
            for ename in engs:
                regs[ename](mkp(ename))

    def ck(self, tag):
        if _STOP_AFTER == tag:
            raise _StopEmission

    def tap(self, name, src_ap, shape, dtype):
        if name not in _TAPS or name in self.tap_names:
            return
        t = self.nc.dram_tensor("tap_" + name, list(shape), dtype,
                                kind="ExternalOutput").ap()
        self.tap_names.append(name)
        self.c["sq"] += 16
        snap = dict(self.c)

        def f(sy):
            for s, v in snap.items():
                if s != "sq":
                    self.w(sy, "sy", s, v)
            sy.dma_start(out=t, in_=src_ap).then_inc(self.sems["sq"], 16)
            self.w(sy, "sy", "sq", snap["sq"])
        self.block.sync(f)
        self.barrier()


def bk(PS, i, w=512):
    return PS[:, i * 512:i * 512 + w]


def emit_proj(p, *, n_oc, wdma, nw, wbatch, lhsT_view, KC, rhs_view, NTC,
              evac, evac_eng="ac", rank1=None, pe_wait0=None, mpart=128):
    PS = p.PS
    nsets = 8 // NTC
    gqb, peb, evb = p.c["gq"], p.c["pe"], p.c[evac_eng]
    p.c["gq"] += 16 * nw
    p.c["pe"] += n_oc
    p.c[evac_eng] += n_oc

    wslf = list(p.wsl_free)
    bso = list(p.bs_owner)

    def gp_f(gp):
        for bi in range(nw):
            if bi >= 2:
                p.w(gp, "gp", "pe", peb + (bi - 1) * wbatch)
            else:
                p.wl(gp, "gp", wslf[bi % 2])
            wdma(gp, bi)
    p.block.gpsimd(gp_f)

    def pe_f(pe):
        if pe_wait0 is not None:
            pe_wait0(pe)
        for i in range(n_oc):
            p.w(pe, "pe", "gq", gqb + 16 * (i // wbatch + 1))
            st = i % nsets
            if i >= nsets:
                p.w(pe, "pe", evac_eng, evb + i - nsets + 1)
            else:
                p.wl(pe, "pe", bso[(st * NTC) // 4])
            first = True
            if rank1 is not None:
                for tc in range(NTC):
                    rank1(pe, i, tc, bk(PS, st * NTC + tc)[:mpart])
                first = False
            ins = None
            for kc in range(KC):
                lt = lhsT_view(i, kc)
                for tc in range(NTC):
                    ins = pe.matmul(bk(PS, st * NTC + tc)[:mpart], lt,
                                    rhs_view(kc, tc),
                                    start=(first and kc == 0),
                                    stop=(kc == KC - 1))
            ins.then_inc(p.sems["pe"], 1)
    p.block.tensor(pe_f)

    def ev_f(eng):
        for i in range(n_oc):
            p.w(eng, evac_eng, "pe", peb + i + 1)
            st = i % nsets
            ps = PS[:mpart, st * NTC * 512:(st * NTC + NTC) * 512]
            evac(eng, i, st, ps).then_inc(p.sems[evac_eng], 1)
    {"ac": p.block.scalar, "dv": p.block.vector}[evac_eng](ev_f)

    p.wsl_free = [[("pe", peb + n_oc)], [("pe", peb + n_oc)]]
    p.bs_owner = [[(evac_eng, evb + n_oc)], [(evac_eng, evb + n_oc)]]


def emit_proj_accum(p, *, n_oc, wdma, nw, wbatch, lhsT_view, KC, rhs_view, NTC,
                    dest_piece, of32, rank1=None, pe_wait0=None):
    PS = p.PS
    nsets = 8 // NTC
    npc = NTC // 2
    gqb, peb, acb = p.c["gq"], p.c["pe"], p.c["ac"]
    p.c["pe"] += n_oc
    p.c["ac"] += npc * n_oc

    gq_w = [0] * nw
    gq_acc = [[0] * npc for _ in range(n_oc)]
    cnt = 0
    order = []
    for i in range(n_oc + 2):
        if i < nw:
            cnt += 1
            gq_w[i] = gqb + 16 * cnt
            order.append(("w", i))
        if 2 <= i < n_oc + 2:
            for pc in range(npc):
                cnt += 1
                gq_acc[i - 2][pc] = gqb + 16 * cnt
                order.append(("a", i - 2, pc))
    p.c["gq"] = gqb + 16 * cnt

    wslf = list(p.wsl_free)
    bso = list(p.bs_owner)
    of32f = list(p.of32_free)

    def gp_f(gp):
        for ev in order:
            if ev[0] == "w":
                bi = ev[1]
                if bi >= 2:
                    p.w(gp, "gp", "pe", peb + (bi - 1) * wbatch)
                else:
                    p.wl(gp, "gp", wslf[bi % 2])
                wdma(gp, bi)
            else:
                _, oc, pc = ev
                j = oc * npc + pc
                p.w(gp, "gp", "ac", acb + j + 1)
                gp.dma_start(out=dest_piece(oc, pc),
                             in_=of32[:, (j % 2) * 1024:(j % 2) * 1024 + 1024],
                             accum_op=ALU.add).then_inc(p.sems["gq"], 16)
    p.block.gpsimd(gp_f)

    def pe_f(pe):
        if pe_wait0 is not None:
            pe_wait0(pe)
        for i in range(n_oc):
            p.w(pe, "pe", "gq", gq_w[i // wbatch])
            st = i % nsets
            if i >= nsets:
                p.w(pe, "pe", "ac", acb + npc * (i - nsets + 1))
            else:
                p.wl(pe, "pe", bso[(st * NTC) // 4])
            first = True
            if rank1 is not None:
                for tc in range(NTC):
                    rank1(pe, i, tc, bk(PS, st * NTC + tc))
                first = False
            ins = None
            for kc in range(KC):
                lt = lhsT_view(i, kc)
                for tc in range(NTC):
                    ins = pe.matmul(bk(PS, st * NTC + tc), lt, rhs_view(kc, tc),
                                    start=(first and kc == 0),
                                    stop=(kc == KC - 1))
            ins.then_inc(p.sems["pe"], 1)
    p.block.tensor(pe_f)

    def ac_f(ac):
        for i in range(n_oc):
            p.w(ac, "ac", "pe", peb + i + 1)
            st = i % nsets
            for pc in range(npc):
                j = i * npc + pc
                if j >= 2:
                    jj = j - 2
                    p.w(ac, "ac", "gq", gq_acc[jj // npc][jj % npc])
                else:
                    p.wl(ac, "ac", of32f[j % 2])
                ac.activation(of32[:, (j % 2) * 1024:(j % 2) * 1024 + 1024],
                              PS[:, (st * NTC + 2 * pc) * 512:(st * NTC + 2 * pc + 2) * 512],
                              AF.Copy).then_inc(p.sems["ac"], 1)
    p.block.scalar(ac_f)

    p.wsl_free = [[("pe", peb + n_oc)], [("pe", peb + n_oc)]]
    p.bs_owner = [[("ac", acb + npc * n_oc)], [("ac", acb + npc * n_oc)]]
    p.of32_free = [[("gq", p.c["gq"])], [("gq", p.c["gq"])]]


def emit_ln(p, t, src_hbm, xb_view, xb_ap3, tok0, ntok, src_ready=None,
            buf_ready=None):
    PS = p.PS
    SC = 256
    nsc = ntok // SC
    sqb, gqb = p.c["sq"], p.c["gq"]
    peb, acb, dvb, pob = p.c["pe"], p.c["ac"], p.c["dv"], p.c["po"]
    p.c["sq"] += 16 * nsc
    p.c["gq"] += 16 * nsc
    p.c["pe"] += 4 * nsc
    p.c["ac"] += 2 * nsc
    p.c["dv"] += 2 * nsc
    p.c["po"] += nsc
    prev = p.ln_prev or {"pe": 0, "ac": 0, "dv": 0, "po": 0}
    bso1 = list(p.bs_owner[1] or [])

    src3 = src_hbm.rearrange("(kc q) n -> q kc n", q=128)

    def sy_f(sy):
        p.wl(sy, "sy", src_ready)
        for sc in range(nsc):
            if sc >= 2:
                p.w(sy, "sy", "dv", dvb + 2 * (sc - 1))
            else:
                p.w(sy, "sy", "dv", prev["dv"])
            c0 = tok0 + sc * SC
            _WLOG.append(("sy", "dma-lnload", tok0, sc))
            sy.dma_start(
                out=t["xf32"][sc % 2].ap().rearrange("q (kc n) -> q kc n", n=SC),
                in_=src3[:, :, c0:c0 + SC]).then_inc(p.sems["sq"], 16)
    p.block.sync(sy_f)

    def gp_f(gp):
        p.wl(gp, "gp", src_ready)
        p.wl(gp, "gp", buf_ready)
        for sc in range(nsc):
            c0 = tok0 + sc * SC
            gp.dma_start(
                out=xb_ap3(sc * SC, SC),
                in_=src3[:, :, c0:c0 + SC]).then_inc(p.sems["gq"], 16)
    p.block.gpsimd(gp_f)

    def pe_f(pe):
        for sc in range(nsc):
            p.w(pe, "pe", "gq", gqb + 16 * (sc + 1))
            if sc >= 1:
                p.w(pe, "pe", "ac", acb + 2 * (sc - 1) + 1)
            else:
                p.wl(pe, "pe", bso1)
                p.wl(pe, "pe", src_ready)
                p.w(pe, "pe", "ac", prev["ac"])
                p.w(pe, "pe", "dv", prev["dv"])
            ins = None
            for kc in range(KCE):
                ins = pe.matmul(bk(PS, 4)[0:1, :SC], t["ones_b"],
                                xb_view(kc, sc * SC, SC),
                                start=(kc == 0), stop=(kc == KCE - 1))
            ins.then_inc(p.sems["pe"], 1)
            p.w(pe, "pe", "po", pob + sc + 1)
            for kc in range(KCE):
                ins = pe.matmul(bk(PS, 5)[0:1, :SC], t["ones_b"],
                                t["xsq"][sc % len(t["xsq"])][:, kc * SC:(kc + 1) * SC],
                                start=(kc == 0), stop=(kc == KCE - 1))
            ins.then_inc(p.sems["pe"], 1)
            p.w(pe, "pe", "ac", acb + 2 * sc + 2)
            if sc >= 1:
                p.w(pe, "pe", "dv", dvb + 2 * sc)
            pe.matmul(bk(PS, 6)[:, :SC], t["ones_fr"], t["m_row"],
                      start=True, stop=True).then_inc(p.sems["pe"], 1)
            pe.matmul(bk(PS, 7)[:, :SC], t["ones_fr"], t["r_row"],
                      start=True, stop=True).then_inc(p.sems["pe"], 1)
    p.block.tensor(pe_f)

    def ac_f(ac):
        for sc in range(nsc):
            par = sc % 2
            p.w(ac, "ac", "gq", gqb + 16 * (sc + 1))
            nq = len(t["xsq"])
            if sc >= nq:
                p.w(ac, "ac", "pe", peb + 4 * (sc - nq) + 2)
            else:
                p.w(ac, "ac", "pe", prev["pe"])
                p.wl(ac, "ac", src_ready)
            ac.activation(t["xsq"][sc % len(t["xsq"])][:, :], xb_ap3(sc * SC, SC),
                          AF.Square).then_inc(p.sems["po"], 1)
            p.w(ac, "ac", "pe", peb + 4 * sc + 2)
            if sc >= 1:
                p.w(ac, "ac", "dv", dvb + 2 * (sc - 1) + 1)
                p.w(ac, "ac", "pe", peb + 4 * sc)
            else:
                p.w(ac, "ac", "dv", prev["dv"])
            ac.activation(t["m_row"], bk(PS, 4)[0:1, :SC], AF.Copy, scale=1.0 / E)
            ac.activation(t["s2_row"], bk(PS, 5)[0:1, :SC], AF.Copy,
                          scale=1.0 / E).then_inc(p.sems["ac"], 1)
            p.w(ac, "ac", "dv", dvb + 2 * sc + 1)
            ac.activation(t["r_row"], t["var_row"], AF.Ln, bias=t["c_eps"])
            ac.activation(t["r_row"], t["r_row"], AF.Exp,
                          scale=-0.5).then_inc(p.sems["ac"], 1)
    p.block.scalar(ac_f)

    def dv_f(dv):
        for sc in range(nsc):
            par = sc % 2
            p.w(dv, "dv", "ac", acb + 2 * sc + 1)
            dv.tensor_tensor(t["var_row"], t["m_row"], t["m_row"], ALU.mult)
            dv.tensor_tensor(t["var_row"], t["s2_row"], t["var_row"],
                             ALU.subtract).then_inc(p.sems["dv"], 1)
            p.w(dv, "dv", "pe", peb + 4 * sc + 4)
            p.w(dv, "dv", "sq", sqb + 16 * (sc + 1))
            ins = None
            _WLOG.append(("dv", "dma-lnnorm", tok0, sc))
            for kc in range(KCE):
                dv.tensor_tensor(t["ts"][:, :SC],
                                 t["xf32"][par][:, kc * SC:(kc + 1) * SC],
                                 bk(PS, 6)[:, :SC], ALU.subtract)
                ins = dv.tensor_tensor(xb_view(kc, sc * SC, SC), t["ts"][:, :SC],
                                       bk(PS, 7)[:, :SC], ALU.mult)
            ins.then_inc(p.sems["dv"], 1)
    p.block.vector(dv_f)

    p.ln_prev = {"pe": peb + 4 * nsc, "ac": acb + 2 * nsc,
                 "dv": dvb + 2 * nsc, "po": pob + nsc}
    p.bs_owner[1] = [("ac", acb + 2 * nsc), ("dv", dvb + 2 * nsc)]


def emit_rope_proj(p, *, wdma, lhsT_view, KC, rhs_view, dest, rp, pe_wait_pv=None,
                   NTC=4, tok0=0):
    PS = p.PS
    gqb, peb, dvb, acb = p.c["gq"], p.c["pe"], p.c["dv"], p.c["ac"]
    p.c["gq"] += 16
    p.c["pe"] += 1 + NTC
    p.c["dv"] += NTC
    p.c["ac"] += NTC

    p.block.gpsimd(wdma)

    bso = list(p.bs_owner)

    def pe_f(pe):
        p.w(pe, "pe", "gq", gqb + 16)
        if pe_wait_pv is not None:
            pe_wait_pv(pe)
        p.wl(pe, "pe", bso[0])
        p.wl(pe, "pe", bso[1])
        ins = None
        for kc in range(KC):
            lt = lhsT_view(kc)
            for tc in range(NTC):
                ins = pe.matmul(bk(PS, tc)[0:64], lt, rhs_view(kc, tc),
                                start=(kc == 0), stop=(kc == KC - 1))
        ins.then_inc(p.sems["pe"], 1)
        for tc in range(NTC):
            par = tc % 2
            p.w(pe, "pe", "dv", dvb + tc + 1)
            pe.matmul(bk(PS, 4 + tc)[0:64], rp["rmix"][:, 0:64],
                      rp["v_sb"][0:64, par * 512:(par + 1) * 512],
                      start=True, stop=False)
            pe.matmul(bk(PS, 4 + tc)[0:64], rp["rmix"][:, 64:128],
                      rp["w_sb"][0:64, par * 512:(par + 1) * 512],
                      start=False, stop=True).then_inc(p.sems["pe"], 1)
    p.block.tensor(pe_f)

    def dv_f(dv):
        for tc in range(NTC):
            par = tc % 2
            p.w(dv, "dv", "pe", peb + 1)
            if tc >= 2:
                p.w(dv, "dv", "pe", peb + 1 + tc - 1)
            else:
                p.w(dv, "dv", "pe", peb)
            dv.tensor_tensor(rp["v_sb"][0:64, par * 512:(par + 1) * 512],
                             bk(PS, tc)[0:64],
                             rp["csk"][:, tok0 + tc * 512:tok0 + (tc + 1) * 512], ALU.mult)
            dv.tensor_tensor(rp["w_sb"][0:64, par * 512:(par + 1) * 512],
                             bk(PS, tc)[0:64],
                             rp["sck"][:, tok0 + tc * 512:tok0 + (tc + 1) * 512],
                             ALU.mult).then_inc(p.sems["dv"], 1)
    p.block.vector(dv_f)

    def ac_f(ac):
        for tc in range(NTC):
            p.w(ac, "ac", "pe", peb + 1 + tc + 1)
            ac.activation(dest[0:64, tok0 + tc * 512:tok0 + (tc + 1) * 512],
                          bk(PS, 4 + tc)[0:64],
                          AF.Copy).then_inc(p.sems["ac"], 1)
    p.block.scalar(ac_f)

    p.bs_owner = [[("dv", dvb + NTC)], [("ac", acb + NTC)]]


def emit_qr_pairs(p, *, wdma_pair, lhsT_view, rhs_view, W, rp2, qstg, qr_scr,
                  c0, sq_free=None):
    PS = p.PS
    gqb, peb, dvb, acb, sqb = p.c["gq"], p.c["pe"], p.c["dv"], p.c["ac"], p.c["sq"]
    p.c["gq"] += 16 * 8
    p.c["pe"] += 16
    p.c["dv"] += 8
    p.c["ac"] += 8
    p.c["sq"] += 16 * 8
    wslf = list(p.wsl_free)
    bso = list(p.bs_owner)

    def gp_f(gp):
        for j in range(8):
            if j >= 2:
                p.w(gp, "gp", "pe", peb + 2 * (j - 1) - 1)
            else:
                p.wl(gp, "gp", wslf[j % 2])
            wdma_pair(gp, j)
    p.block.gpsimd(gp_f)

    def pe_f(pe):
        for j in range(8):
            p.w(pe, "pe", "gq", gqb + 16 * (j + 1))
            if j >= 4:
                p.w(pe, "pe", "dv", dvb + j - 3)
            else:
                p.wl(pe, "pe", bso[0])
            ins = None
            for kc in range(KCE):
                ins = pe.matmul(bk(PS, j % 4)[:, 0:W], lhsT_view(j, kc),
                                rhs_view(kc), start=(kc == 0), stop=(kc == KCE - 1))
            ins.then_inc(p.sems["pe"], 1)
            p.w(pe, "pe", "dv", dvb + j + 1)
            if j >= 4:
                p.w(pe, "pe", "ac", acb + j - 3)
            else:
                p.wl(pe, "pe", bso[1])
            pe.matmul(bk(PS, 4 + j % 4)[:, 0:W], rp2["rmix"][:, 0:128],
                      rp2["v_sb"][:, (j % 2) * 512:(j % 2) * 512 + W],
                      start=True, stop=False)
            pe.matmul(bk(PS, 4 + j % 4)[:, 0:W], rp2["rmix"][:, 128:256],
                      rp2["w_sb"][:, (j % 2) * 512:(j % 2) * 512 + W],
                      start=False, stop=True).then_inc(p.sems["pe"], 1)
    p.block.tensor(pe_f)

    def dv_f(dv):
        for j in range(8):
            p.w(dv, "dv", "pe", peb + 2 * j + 1)
            if j < 2:
                p.w(dv, "dv", "pe", peb)
            dv.tensor_tensor(rp2["v_sb"][:, (j % 2) * 512:(j % 2) * 512 + W],
                             bk(PS, j % 4)[:, 0:W],
                             rp2["csk"][:, c0:c0 + W], ALU.mult)
            dv.tensor_tensor(rp2["w_sb"][:, (j % 2) * 512:(j % 2) * 512 + W],
                             bk(PS, j % 4)[:, 0:W],
                             rp2["sck"][:, c0:c0 + W],
                             ALU.mult).then_inc(p.sems["dv"], 1)
    p.block.vector(dv_f)

    def ac_f(ac):
        for j in range(8):
            p.w(ac, "ac", "pe", peb + 2 * j + 2)
            if j >= 2:
                p.w(ac, "ac", "sq", sqb + 16 * (j - 1))
            elif sq_free is not None:
                p.wl(ac, "ac", sq_free[j % 2])
            ac.activation(qstg[:, (j % 2) * 512:(j % 2) * 512 + W],
                          bk(PS, 4 + j % 4)[:, 0:W],
                          AF.Copy).then_inc(p.sems["ac"], 1)
    p.block.scalar(ac_f)

    def sy_f(sy):
        for j in range(8):
            p.w(sy, "sy", "ac", acb + j + 1)
            sy.dma_start(out=qr_scr[j * 128:(j + 1) * 128, c0:c0 + W],
                         in_=qstg[:, (j % 2) * 512:(j % 2) * 512 + W]
                         ).then_inc(p.sems["sq"], 16)
    p.block.sync(sy_f)

    p.wsl_free = [[("pe", peb + 16)], [("pe", peb + 16)]]
    p.bs_owner = [[("dv", dvb + 8)], [("ac", acb + 8)]]
    return [("sq", sqb + 16 * 7)], [("sq", sqb + 16 * 8)]


def emit_attn_a_head(p, h, hb, ten):
    PS = p.PS
    gqb, sqb = p.c["gq"], p.c["sq"]
    peb, acb, dvb = p.c["pe"], p.c["ac"], p.c["dv"]
    ntks = [4 * qb + 4 for qb in range(4)]
    PEB, ACB, DVB = 5, 5, 0
    pe_qb0 = [peb + PEB + sum(nt + 3 for nt in ntks[:qb]) for qb in range(5)]
    ac_qb0 = [acb + ACB + sum(nt + 2 for nt in ntks[:qb]) for qb in range(5)]
    dv_qb0 = [dvb + DVB + 5 * qb for qb in range(5)]
    p.c["gq"] += 16 * 2
    p.c["sq"] += 16 * 2 + 16 * 4
    p.c["pe"] = pe_qb0[4]
    p.c["ac"] = ac_qb0[4]
    p.c["dv"] = dv_qb0[4]
    sqq = sqb + 32
    ao_sq0 = sqb + 32

    scores_inc = lambda qb, tk: pe_qb0[qb] + tk + 1
    sums_stop = lambda qb: pe_qb0[qb] + ntks[qb] + 1
    attnv_stop = lambda qb: pe_qb0[qb] + ntks[qb] + 2
    rsbc_inc = lambda qb: pe_qb0[qb] + ntks[qb] + 3
    exp_inc = lambda qb, tk: ac_qb0[qb] + tk + 1
    sumsln_inc = lambda qb: ac_qb0[qb] + ntks[qb] + 1
    rscopy_inc = lambda qb: ac_qb0[qb] + ntks[qb] + 2
    mask_inc = lambda qb, d: dv_qb0[qb] + d + 1
    aoev_inc = lambda qb: dv_qb0[qb] + 5
    par2 = lambda qb: qb % 2

    sl = h % 2
    qch, qrh = ten["qch2"][sl], ten["qrh2"][sl]
    kh, vtm, probs = ten["kh"], ten["vtm"], ten["probs"]
    ckv, kR = ten["ckv"], ten["kR"]
    bso = [list(p.bs_owner[0] or []), list(p.bs_owner[1] or [])]

    def sy_q(sy):
        if hb.get("hfin") and len(hb["hfin"]) >= 2:
            p.w(sy, "sy", "pe", hb["hfin"][-2])
        elif hb.get("qfree") is not None:
            p.w(sy, "sy", "pe", hb["qfree"])
        sy.dma_start(out=qch[:, :],
                     in_=ten["qc_scr"][h * 128:(h + 1) * 128, :]).then_inc(p.sems["sq"], 16)
        sy.dma_start(out=qrh[0:64, :],
                     in_=ten["qr_scr"][h * 64:(h + 1) * 64, :]).then_inc(p.sems["sq"], 16)
    p.block.sync(sy_q)

    def gp_f(gp):
        if hb.get("wfree") is not None:
            p.w(gp, "gp", "pe", hb["wfree"])
        gp.dma_start(out=ten["wuk_sb"][sl].ap().rearrange("q (k m) -> q k m", m=128),
                     in_=ten["wuk"][h]).then_inc(p.sems["gq"], 16)
        gp.dma_start(out=ten["wuv_sb"][sl].ap().rearrange("q (k m) -> q k m", m=128),
                     in_=ten["wuvr"][h].rearrange("k q m -> q k m")).then_inc(p.sems["gq"], 16)
    p.block.gpsimd(gp_f)
    hb["wfree"] = peb + PEB
    hb.setdefault("hfin", []).append(pe_qb0[4])

    def pe_f(pe):
        p.w(pe, "pe", "gq", gqb + 32)
        if h > 0:
            p.w(pe, "pe", "ac", acb)
            p.w(pe, "pe", "dv", dvb)
        else:
            p.wl(pe, "pe", bso[0])
            p.wl(pe, "pe", bso[1])
        ins = None
        for kc in range(4):
            for tc in range(4):
                ins = pe.matmul(bk(PS, tc), ten["wuk_sb"][sl][:, kc * 128:(kc + 1) * 128],
                                ckv[:, kc * S + tc * 512: kc * S + (tc + 1) * 512],
                                start=(kc == 0), stop=(kc == 3))
        ins.then_inc(p.sems["pe"], 1)
        for mg in range(4):
            for mm in range(4):
                m = mg * 4 + mm
                for kc in range(4):
                    ins = pe.matmul(bk(PS, 4 + mg)[:, mm * 128:(mm + 1) * 128],
                                    ckv[:, kc * S + m * 128: kc * S + (m + 1) * 128],
                                    ten["wuv_sb"][sl][:, kc * 128:(kc + 1) * 128],
                                    start=(kc == 0), stop=(kc == 3))
            ins.then_inc(p.sems["pe"], 1)
        p.w(pe, "pe", "sq", sqq)
        for qb in range(4):
            ntk = ntks[qb]
            qsl = slice(qb * 512, (qb + 1) * 512)

            def scores(tk):
                if tk >= 2:
                    p.w(pe, "pe", "ac", exp_inc(qb, tk - 2))
                else:
                    p.w(pe, "pe", "ac", ac_qb0[qb])
                pe.matmul(bk(PS, tk % 2), kh[:, tk * 128:(tk + 1) * 128],
                          qch[:, qsl], start=True, stop=False)
                pe.matmul(bk(PS, tk % 2), kR[0:64, tk * 128:(tk + 1) * 128],
                          qrh[0:64, qsl], start=False,
                          stop=True).then_inc(p.sems["pe"], 1)

            def consume(tk):
                if tk >= 4 * qb:
                    p.w(pe, "pe", "dv", mask_inc(qb, tk - 4 * qb))
                else:
                    p.w(pe, "pe", "ac", exp_inc(qb, tk))
                if tk == 0 and qb >= 2:
                    p.w(pe, "pe", "ac", sumsln_inc(qb - 2))
                    p.w(pe, "pe", "dv", aoev_inc(qb - 2))
                pr = probs[:, tk * 512:(tk + 1) * 512]
                ins0 = pe.matmul(bk(PS, 2 + par2(qb))[0:1], ten["ones_b"], pr,
                                 start=(tk == 0), stop=(tk == ntk - 1))
                if tk == ntk - 1:
                    ins0.then_inc(p.sems["pe"], 1)
                ins1 = pe.matmul(bk(PS, 4 + par2(qb)), vtm[:, tk * 128:(tk + 1) * 128],
                                 pr, start=(tk == 0), stop=(tk == ntk - 1))
                if tk == ntk - 1:
                    ins1.then_inc(p.sems["pe"], 1)

            scores(0)
            if ntk > 1:
                scores(1)
            for tk in range(ntk):
                consume(tk)
                if tk + 2 < ntk:
                    scores(tk + 2)
            p.w(pe, "pe", "ac", sumsln_inc(qb))
            if qb >= 2:
                p.w(pe, "pe", "ac", rscopy_inc(qb - 2))
            pe.matmul(bk(PS, 6 + par2(qb)), ten["ones_fr"], ten["rs_row"],
                      start=True, stop=True).then_inc(p.sems["pe"], 1)
    p.block.tensor(pe_f)

    def ac_f(ac):
        p.w(ac, "ac", "pe", peb + 1)
        ac.activation(kh[:, :], PS[:, 0:2048], AF.Copy).then_inc(p.sems["ac"], 1)
        for mg in range(4):
            p.w(ac, "ac", "pe", peb + 2 + mg)
            ac.activation(vtm[:, mg * 512:(mg + 1) * 512], bk(PS, 4 + mg),
                          AF.Copy).then_inc(p.sems["ac"], 1)
        for qb in range(4):
            ntk = ntks[qb]
            for tk in range(ntk):
                p.w(ac, "ac", "pe", scores_inc(qb, tk))
                ac.activation(probs[:, tk * 512:(tk + 1) * 512], bk(PS, tk % 2),
                              AF.Exp, bias=ten["c_negB"],
                              scale=SCALE_A).then_inc(p.sems["ac"], 1)
            p.w(ac, "ac", "pe", attnv_stop(qb))
            if qb >= 1:
                p.w(ac, "ac", "pe", rsbc_inc(qb - 1))
            ac.activation(ten["rs_row"], bk(PS, 2 + par2(qb))[0:1], AF.Ln,
                          bias=ten["c_zero"])
            ac.activation(ten["rs_row"], ten["rs_row"], AF.Exp,
                          scale=-1.0).then_inc(p.sems["ac"], 1)
            p.w(ac, "ac", "pe", rsbc_inc(qb))
            if qb >= 2:
                p.w(ac, "ac", "dv", aoev_inc(qb - 2))
            ac.activation(ten["rs_sb"][:, par2(qb) * 512:(par2(qb) + 1) * 512],
                          bk(PS, 6 + par2(qb)), AF.Copy).then_inc(p.sems["ac"], 1)
    p.block.scalar(ac_f)

    def dv_f(dv):
        for qb in range(4):
            for d in range(4):
                tk = 4 * qb + d
                p.w(dv, "dv", "ac", exp_inc(qb, tk))
                dv.tensor_tensor(probs[:, tk * 512:(tk + 1) * 512],
                                 probs[:, tk * 512:(tk + 1) * 512],
                                 ten["maska_sb"][:, d * 512:(d + 1) * 512],
                                 ALU.mult).then_inc(p.sems["dv"], 1)
            p.w(dv, "dv", "pe", attnv_stop(qb))
            p.w(dv, "dv", "ac", rscopy_inc(qb))
            if qb >= 2:
                p.w(dv, "dv", "sq", ao_sq0 + 16 * (qb - 1))
            elif hb.get("aofree") is not None:
                p.w(dv, "dv", "sq", hb["aofree"][par2(qb)])
            dv.tensor_tensor(ten["aosl"][:, par2(qb) * 512:(par2(qb) + 1) * 512],
                             bk(PS, 4 + par2(qb)),
                             ten["rs_sb"][:, par2(qb) * 512:(par2(qb) + 1) * 512],
                             ALU.mult).then_inc(p.sems["dv"], 1)
    p.block.vector(dv_f)

    def sy_f(sy):
        for qb in range(4):
            p.w(sy, "sy", "dv", aoev_inc(qb))
            sy.dma_start(out=ten["ao_scr"][h * 128:(h + 1) * 128, qb * 512:(qb + 1) * 512],
                         in_=ten["aosl"][:, par2(qb) * 512:(par2(qb) + 1) * 512]
                         ).then_inc(p.sems["sq"], 16)
    p.block.sync(sy_f)
    hb["aofree"] = [ao_sq0 + 16 * 3, ao_sq0 + 16 * 4]


def emit_attn_b_group(p, ten):
    PS = p.PS
    peb, acb, dvb = p.c["pe"], p.c["ac"], p.c["dv"]
    Qg, Kg, vtm, ao_g, probs = ten["Qg"], ten["Kg"], ten["vtm_g"], ten["ao_g"], ten["probs"]

    plans = []
    for hl in range(8):
        for qb in range(16):
            tiles = MASKB_TILES[qb]
            chunks = [tiles[i:i + 4] for i in range(0, len(tiles), 4)]
            plans.append((hl, qb, tiles, chunks))

    pe_at, ac_at, dv_at = [], [], []
    cpe, cac, cdv = peb, acb, dvb
    for (hl, qb, tiles, chunks) in plans:
        mchunks = sum(1 for ch in chunks if any(mi is not None for _, mi in ch))
        pe_at.append(cpe)
        ac_at.append(cac)
        dv_at.append((cdv, mchunks))
        cpe += len(chunks) + 3
        cac += len(chunks) + 2
        cdv += mchunks + 1
    p.c["pe"], p.c["ac"], p.c["dv"] = cpe, cac, cdv

    def sc_inc(i, c):
        return pe_at[i] + c + 1

    def sums_stop(i):
        return pe_at[i] + len(plans[i][3]) + 1

    def attnv_stop(i):
        return pe_at[i] + len(plans[i][3]) + 2

    def rsbc_inc(i):
        return pe_at[i] + len(plans[i][3]) + 3

    def exp_inc(i, c):
        return ac_at[i] + c + 1

    def sumsln_inc(i):
        return ac_at[i] + len(plans[i][3]) + 1

    def rscopy_inc(i):
        return ac_at[i] + len(plans[i][3]) + 2

    def mask_inc(i, mc):
        return dv_at[i][0] + mc + 1

    def aoev_inc(i):
        return dv_at[i][0] + dv_at[i][1] + 1

    def probs_ready(i, c):
        ch = plans[i][3][c]
        if any(mi is not None for _, mi in ch):
            mc = sum(1 for cc in plans[i][3][:c + 1]
                     if any(mi is not None for _, mi in cc))
            return ("dv", mask_inc(i, mc - 1))
        return ("ac", exp_inc(i, c))

    bso = [list(p.bs_owner[0] or []), list(p.bs_owner[1] or [])]

    def pe_f(pe):
        for i, (hl, qb, tiles, chunks) in enumerate(plans):
            par = i % 2
            qv = Qg[:, hl * S + qb * 128: hl * S + (qb + 1) * 128]
            if i == 0:
                p.wl(pe, "pe", bso[0])
                p.wl(pe, "pe", bso[1])
            for c, ch in enumerate(chunks):
                if c >= 2:
                    p.w(pe, "pe", "ac", exp_inc(i, c - 2))
                elif i >= 1:
                    p.w(pe, "pe", "ac", ac_at[i])
                ins = None
                for jj, (tk, mi) in enumerate(ch):
                    ins = pe.matmul(bk(PS, c % 2)[:, jj * 128:(jj + 1) * 128],
                                    Kg[:, hl * S + tk * 128: hl * S + (tk + 1) * 128],
                                    qv, start=True, stop=True)
                ins.then_inc(p.sems["pe"], 1)
            first = True
            nt = len(tiles)
            ti = 0
            for c, ch in enumerate(chunks):
                sem, val = probs_ready(i, c)
                p.w(pe, "pe", sem, val)
                if first and i >= 2:
                    p.w(pe, "pe", "ac", sumsln_inc(i - 2))
                    p.w(pe, "pe", "dv", aoev_inc(i - 2))
                for jj, (tk, mi) in enumerate(ch):
                    pr = probs[:, (4 * c + jj) * 128:(4 * c + jj + 1) * 128]
                    last = (ti == nt - 1)
                    ins0 = pe.matmul(bk(PS, 2 + par)[0:1, 0:128], ten["ones_b"],
                                     pr, start=first, stop=last)
                    if last:
                        ins0.then_inc(p.sems["pe"], 1)
                    ins1 = pe.matmul(bk(PS, 4 + par)[:, 0:128],
                                     vtm[:, (tk * 8 + hl) * 128:(tk * 8 + hl + 1) * 128],
                                     pr, start=first, stop=last)
                    if last:
                        ins1.then_inc(p.sems["pe"], 1)
                    first = False
                    ti += 1
            p.w(pe, "pe", "ac", sumsln_inc(i))
            if i >= 2:
                p.w(pe, "pe", "ac", rscopy_inc(i - 2))
            pe.matmul(bk(PS, 6 + par)[:, 0:128], ten["ones_fr"],
                      ten["rs_row"][0:1, 0:128], start=True,
                      stop=True).then_inc(p.sems["pe"], 1)
    p.block.tensor(pe_f)

    def ac_f(ac):
        for i, (hl, qb, tiles, chunks) in enumerate(plans):
            par = i % 2
            for c, ch in enumerate(chunks):
                p.w(ac, "ac", "pe", sc_inc(i, c))
                if i >= 1:
                    p.w(ac, "ac", "pe", attnv_stop(i - 1))
                ac.activation(probs[:, 4 * c * 128:(4 * c + len(ch)) * 128],
                              bk(PS, c % 2)[:, 0:len(ch) * 128], AF.Exp,
                              bias=ten["c_negB"],
                              scale=SCALE_B).then_inc(p.sems["ac"], 1)
            p.w(ac, "ac", "pe", attnv_stop(i))
            if i >= 1:
                p.w(ac, "ac", "pe", rsbc_inc(i - 1))
            ac.activation(ten["rs_row"][0:1, 0:128], bk(PS, 2 + par)[0:1, 0:128],
                          AF.Ln, bias=ten["c_zero"])
            ac.activation(ten["rs_row"][0:1, 0:128], ten["rs_row"][0:1, 0:128],
                          AF.Exp, scale=-1.0).then_inc(p.sems["ac"], 1)
            p.w(ac, "ac", "pe", rsbc_inc(i))
            if i >= 2:
                p.w(ac, "ac", "dv", aoev_inc(i - 2))
            ac.activation(ten["rs_sb"][:, par * 512:par * 512 + 128],
                          bk(PS, 6 + par)[:, 0:128], AF.Copy).then_inc(p.sems["ac"], 1)
    p.block.scalar(ac_f)

    def dv_f(dv):
        for i, (hl, qb, tiles, chunks) in enumerate(plans):
            par = i % 2
            for c, ch in enumerate(chunks):
                if not any(mi is not None for _, mi in ch):
                    continue
                p.w(dv, "dv", "ac", exp_inc(i, c))
                ins = None
                for jj, (tk, mi) in enumerate(ch):
                    if mi is None:
                        continue
                    pr = probs[:, (4 * c + jj) * 128:(4 * c + jj + 1) * 128]
                    ins = dv.tensor_tensor(pr, pr,
                                           ten["maskb_sb"][:, mi * 128:(mi + 1) * 128],
                                           ALU.mult)
                ins.then_inc(p.sems["dv"], 1)
            p.w(dv, "dv", "pe", attnv_stop(i))
            p.w(dv, "dv", "ac", rscopy_inc(i))
            dv.tensor_tensor(ao_g[:, hl * S + qb * 128: hl * S + (qb + 1) * 128],
                             bk(PS, 4 + par)[:, 0:128],
                             ten["rs_sb"][:, par * 512:par * 512 + 128],
                             ALU.mult).then_inc(p.sems["dv"], 1)
    p.block.vector(dv_f)



def build_nc():
    nc = bass.Bass("TRN2", target_bir_lowering=False)
    ap_ = lambda name, shape, dt=F32: nc.dram_tensor(name, list(shape), dt,
                                                     kind="ExternalInput").ap()
    x_in = ap_("x", [NSEQ, E, S])
    wdq = ap_("wdq", [16, 128, 16, 128]); wdkv = ap_("wdkv", [4, 128, 16, 128])
    wkr = ap_("wkr", [1, 128, 16, 64]); wuq = ap_("wuq", [16, 128, 16, 128])
    wqr2 = ap_("wqr2", [8, 128, 16, 128]); wuk = ap_("wuk", [16, 128, 4, 128])
    wuvr = ap_("wuvr", [16, 4, 128, 128]); wo = ap_("wo", [16, 128, 16, 128])
    wq = ap_("wq", [16, 128, 16, 128]); wk = ap_("wk", [16, 128, 16, 128])
    wvr = ap_("wvr", [2, 16, 128, 1024]); woo = ap_("woo", [16, 128, 16, 128])
    w1 = ap_("w1", [32, 128, 16, 128]); w2 = ap_("w2", [32, 128, 32, 128])
    w3 = ap_("w3", [16, 128, 32, 128])
    b1c = ap_("b1c", [128, 32]); b2c = ap_("b2c", [128, 32]); b3r = ap_("b3r", [1, E])
    csk_h = ap_("csk2", [128, S]); sck_h = ap_("sck2", [128, S])
    rmix_h = ap_("rmix", [64, 128], BF16)
    rmix2_h = ap_("rmix2", [128, 256], BF16)
    maska_h = ap_("maska", [4, 128, 512], BF16)
    maskb_h = ap_("maskb", [NMASKB, 128, 128], BF16)

    y_out = nc.dram_tensor("y", [NSEQ, E, S], F32, kind="ExternalOutput").ap()
    X_int = nc.dram_tensor("Xint", [NSEQ, E, S], F32, kind="Internal").ap()
    ao_scr = nc.dram_tensor("aoscr", [E, S], BF16, kind="Internal").ap()
    qc_scr = nc.dram_tensor("qcscr", [E, S], BF16, kind="Internal").ap()
    qr_scr = nc.dram_tensor("qrscr", [1024, S], BF16, kind="Internal").ap()

    top = contextlib.ExitStack()
    if True:
        En = top.enter_context
        PS = En(nc.psum_tensor("PS", [128, 4096], F32))
        ones_b = En(nc.sbuf_tensor("ones_b", [128, 1], BF16))
        ones_fr = En(nc.sbuf_tensor("ones_fr", [1, 128], F32))
        ones_br = En(nc.sbuf_tensor("ones_br", [1, 512], BF16))
        c_negB = En(nc.sbuf_tensor("c_negB", [128, 1], F32))
        c_eps = En(nc.sbuf_tensor("c_eps", [1, 1], F32))
        c_zero = En(nc.sbuf_tensor("c_zero", [1, 1], F32))
        b1_sb = En(nc.sbuf_tensor("b1_sb", [128, 32], F32))
        b2_sb = En(nc.sbuf_tensor("b2_sb", [128, 32], F32))
        b3_sb = En(nc.sbuf_tensor("b3_sb", [1, E], BF16))
        maska_sb = En(nc.sbuf_tensor("maska_sb", [128, 4 * 512], BF16))
        maskb_sb = En(nc.sbuf_tensor("maskb_sb", [128, NMASKB * 128], BF16))
        rs_sb = En(nc.sbuf_tensor("rs_sb", [128, 2 * 512], F32))
        rows_sb = En(nc.sbuf_tensor("rows_sb", [1, 1536], F32))
        ts_sb = En(nc.sbuf_tensor("ts_sb", [128, 256], F32))
        rmix_sb = En(nc.sbuf_tensor("rmix_sb", [64, 128], BF16))
        rmix2_sb = En(nc.sbuf_tensor("rmix2_sb", [128, 256], BF16))
        wslA = En(nc.sbuf_tensor("wslA", [128, 4096], BF16))
        wslB = En(nc.sbuf_tensor("wslB", [128, 4096], BF16))

        sems = {s: En(nc.semaphore(s)) for s in ["gq", "sq", "pe", "dv", "ac", "po"]}
        block = En(nc.Block())
        p = P(nc, block, sems)
        p.PS = PS[:, :]
        wsl = [wslA, wslB]
        _uid = [0]

        import functools

        def sb(nm, shape, dtype, **kw):
            _uid[0] += 1
            cm = nc.sbuf_tensor(f"{nm}_{_uid[0]}", shape, dtype, **kw)

            class _Wrap:
                def __enter__(self):
                    h = cm.__enter__()
                    try:
                        mloc = nc.lookup_mloc(h)
                        per = 1
                        for d in h.shape[1:]:
                            per *= d
                        per *= mybir.dt.size(h.dtype)
                        _ALLOCLOG.append(("A", h.name, mloc.addr, per))
                    except Exception as e:
                        _ALLOCLOG.append(("A?", f"{nm}_{_uid[0]}", str(e), 0))
                    self._h = h
                    return h

                def __exit__(self, *a):
                    _ALLOCLOG.append(("F", self._h.name, 0, 0))
                    return cm.__exit__(*a)
            return _Wrap()

        ln_base = {
            "ones_b": ones_b[:, :], "ones_fr": ones_fr[:, :],
            "m_row": rows_sb[0:1, 0:256], "s2_row": rows_sb[0:1, 256:512],
            "var_row": rows_sb[0:1, 512:768], "r_row": rows_sb[0:1, 768:1024],
            "ts": ts_sb[:, :], "c_eps": c_eps[0:1, 0:1],
        }
        rs_row = rows_sb[0:1, 1024:1536]

        p.c["sq"] += 16 * 6

        def init_sy(sy):
            sy.dma_start(out=b1_sb[:, :], in_=b1c).then_inc(sems["sq"], 16)
            sy.dma_start(out=b2_sb[:, :], in_=b2c).then_inc(sems["sq"], 16)
            sy.dma_start(out=maska_sb.ap().rearrange("q (d n) -> q d n", n=512),
                         in_=maska_h.rearrange("d q n -> q d n")).then_inc(sems["sq"], 16)
            sy.dma_start(out=maskb_sb.ap().rearrange("q (d n) -> q d n", n=128),
                         in_=maskb_h.rearrange("d q n -> q d n")).then_inc(sems["sq"], 16)
            sy.dma_start(out=rmix_sb[:, :], in_=rmix_h).then_inc(sems["sq"], 16)
            sy.dma_start(out=rmix2_sb[:, :], in_=rmix2_h).then_inc(sems["sq"], 16)
        block.sync(init_sy)

        p.c["gq"] += 16 * (1 + NSEQ)

        def init_gp(gp):
            gp.dma_start(out=b3_sb[:, :], in_=b3r).then_inc(sems["gq"], 16)
            for sq in range(NSEQ):
                gp.dma_start(out=X_int[sq], in_=x_in[sq]).then_inc(sems["gq"], 16)
        block.gpsimd(init_gp)

        p.c["dv"] += 1

        def init_dv(dv):
            dv.memset(ones_b[:, :], 1.0)
            dv.memset(ones_fr[:, :], 1.0)
            dv.memset(ones_br[:, :], 1.0)
            dv.memset(c_negB[:, :], NEG_EXP_BIAS)
            dv.memset(c_zero[:, :], 0.0)
            dv.memset(c_eps[:, :], EPS).then_inc(sems["dv"], 1)
        block.vector(init_dv)
        p.barrier()
        if _STOP_AFTER == "init":
            snap0 = dict(p.c)

            def fin0(sy):
                for sname, v in snap0.items():
                    p.w(sy, "sy", sname, v)
            block.sync(fin0)
            block.__exit__(None, None, None)
            nc._top_stack = top
            return nc, p

        def alloc_ln(stk, nq=2):
            t = dict(ln_base)
            t["xf32"] = [stk.enter_context(sb(f"ln_xf{i}", [128, KCE * 256], F32, side="right")) for i in range(2)]
            t["xsq"] = [stk.enter_context(sb(f"ln_xq{i}", [128, KCE * 256], BF16, side="right")) for i in range(nq)]
            return t

        def std_wsl(ph, n=16):
            return [ph.enter_context(sb(f"wsl{i}", [128, n * 128], BF16))
                    for i in range(2)]

        def ffn_app(sq, dest_hbm, src_ready=None):
            Xs = X_int[sq]
            for ch in range(2):
                c0 = ch * 1024
                with contextlib.ExitStack() as stk:
                    with contextlib.ExitStack() as xstk:
                        xbc = xstk.enter_context(sb("xbc", [128, KCE * 1024], BF16,
                                                                side="right"))
                        with contextlib.ExitStack() as lns:
                            t = alloc_ln(lns)
                            emit_ln(p, t, Xs,
                                    lambda kc, cc, w: xbc[:, kc * 1024 + cc: kc * 1024 + cc + w],
                                    lambda cc, w: xbc.ap().rearrange("q (kc n) -> q kc n", n=1024)[:, :, cc:cc + w],
                                    c0, 1024, src_ready=src_ready,
                                    buf_ready=[("pe", p.c["pe"]),
                                               ("sq", p.c["sq"])])
                            ln_dv = p.ln_prev["dv"]
                            p.mb()
                            if ch == 1:
                                p.tap("fxb", xbc[:, :], [128, KCE * 1024], BF16)
                                p.tap("fxf0", t["xf32"][0][:, :], [128, KCE * 256], F32)
                                p.tap("fxq0", t["xsq"][0][:, :], [128, KCE * 256], BF16)
                        h1 = stk.enter_context(sb("h1", [128, 32 * 1024], BF16))

                        def dma_w1(gp, bi):
                            gp.dma_start(out=wsl[bi % 2].ap().rearrange(
                                "q (k m) -> q k m", m=128)[:, 0:16, :],
                                in_=w1[bi]).then_inc(sems["gq"], 16)
                        emit_proj(
                            p, n_oc=32, wdma=dma_w1, nw=32, wbatch=1,
                            lhsT_view=lambda oc, kc: wsl[oc % 2][:, kc * 128:(kc + 1) * 128],
                            KC=KCE,
                            rhs_view=lambda kc, tc: xbc[:, kc * 1024 + tc * 512: kc * 1024 + (tc + 1) * 512],
                            NTC=2,
                            evac=lambda eng, oc, st_, ps: eng.activation(
                                h1[:, oc * 1024:(oc + 1) * 1024], ps, AF.Relu,
                                bias=b1_sb[:, oc:oc + 1]),
                            evac_eng="ac",
                            pe_wait0=lambda pe: p.w(pe, "pe", "dv", ln_dv))
                        h1_ac = p.c["ac"]
                        p.mb()
                        p.tap(f"fh1c{ch}", h1[:, :], [128, 32 * 1024], BF16)
                    h2 = stk.enter_context(sb("h2", [128, 32 * 1024], BF16))

                    def dma_w2(gp, bi):
                        gp.dma_start(out=wsl[bi % 2].ap().rearrange(
                            "q (k m) -> q k m", m=128), in_=w2[bi]).then_inc(sems["gq"], 16)
                    emit_proj(
                        p, n_oc=32, wdma=dma_w2, nw=32, wbatch=1,
                        lhsT_view=lambda oc, kc: wsl[oc % 2][:, kc * 128:(kc + 1) * 128],
                        KC=32,
                        rhs_view=lambda kc, tc: h1[:, kc * 1024 + tc * 512: kc * 1024 + (tc + 1) * 512],
                        NTC=2,
                        evac=lambda eng, oc, st_, ps: eng.activation(
                            h2[:, oc * 1024:(oc + 1) * 1024], ps, AF.Relu,
                            bias=b2_sb[:, oc:oc + 1]),
                        evac_eng="ac",
                        pe_wait0=(lambda v: lambda pe: p.w(pe, "pe", "ac", v))(h1_ac))
                    h2_ac = p.c["ac"]
                    p.mb()
                    p.tap(f"fh2c{ch}", h2[:, :], [128, 32 * 1024], BF16)

                    of32f = stk.enter_context(sb("of32f", [128, 2 * 1024], F32))

                    def dma_w3(gp, bi):
                        gp.dma_start(out=wsl[bi % 2].ap().rearrange(
                            "q (k m) -> q k m", m=128), in_=w3[bi]).then_inc(sems["gq"], 16)

                    def rank1_b3(pe, oc, tc, psb):
                        pe.matmul(psb, b3_sb[0:1, oc * 128:(oc + 1) * 128],
                                  ones_br[0:1, 0:512], start=True, stop=False)
                    emit_proj_accum(
                        p, n_oc=16, wdma=dma_w3, nw=16, wbatch=1,
                        lhsT_view=lambda oc, kc: wsl[oc % 2][:, kc * 128:(kc + 1) * 128],
                        KC=32,
                        rhs_view=lambda kc, tc: h2[:, kc * 1024 + tc * 512: kc * 1024 + (tc + 1) * 512],
                        NTC=2,
                        dest_piece=lambda oc, pc: dest_hbm[oc * 128:(oc + 1) * 128, c0:c0 + 1024],
                        of32=of32f[:, :], rank1=rank1_b3,
                        pe_wait0=(lambda v: lambda pe: p.w(pe, "pe", "ac", v))(h2_ac))
                    p.mb()
                src_ready = [("gq", p.c["gq"])]

        try:
          for sq in range(NSEQ):
            Xs = X_int[sq]

            sl1_src_ready = [("gq", p.c["gq"])]
            with contextlib.ExitStack() as sl:
                csk2_sb = sl.enter_context(sb("csk2s", [128, S], F32))
                sck2_sb = sl.enter_context(sb("sck2s", [128, S], F32))
                v2_sb = sl.enter_context(sb("v2s", [128, 2 * 512], BF16))
                w2_sb = sl.enter_context(sb("w2s", [128, 2 * 512], BF16))
                qstg = sl.enter_context(sb("qstg", [128, 2 * 512], BF16))
                qstg2 = sl.enter_context(sb("qstg2", [128, 2 * 512], BF16))
                rp2 = {"csk": csk2_sb[:, :], "sck": sck2_sb[:, :],
                       "rmix": rmix2_sb[:, :], "v_sb": v2_sb[:, :], "w_sb": w2_sb[:, :]}
                rp = {"csk": csk2_sb[0:64, :], "sck": sck2_sb[0:64, :],
                      "rmix": rmix_sb[:, :], "v_sb": v2_sb[0:64, 0:1024],
                      "w_sb": w2_sb[0:64, 0:1024]}
                p.c["sq"] += 32
                pe_now = p.c["pe"]

                def rope_ld(sy):
                    p.w(sy, "sy", "pe", pe_now)
                    sy.dma_start(out=csk2_sb[:, :], in_=csk_h).then_inc(sems["sq"], 16)
                    sy.dma_start(out=sck2_sb[:, :], in_=sck_h).then_inc(sems["sq"], 16)
                block.sync(rope_ld)

                qstk = contextlib.ExitStack()
                ckv = qstk.enter_context(sb("ckv", [128, 4 * S], BF16, side="right"))
                kR = qstk.enter_context(sb("kR", [64, S], BF16, side="right"))
                qr_free = None
                qc_free = [None, None]
                for ch in range(2):
                  c0 = ch * 1024
                  with contextlib.ExitStack() as ph:
                    xb = ph.enter_context(sb("xb", [128, KCE * 1024], BF16))
                    cqc = ph.enter_context(sb("cqc", [128, KCE * 512], BF16))
                    with contextlib.ExitStack() as lns:
                        t = alloc_ln(lns)
                        emit_ln(p, t, Xs,
                                lambda kc, cc, w: xb[:, kc * 1024 + cc: kc * 1024 + cc + w],
                                lambda cc, w: xb.ap().rearrange("q (kc n) -> q kc n", n=1024)[:, :, cc:cc + w],
                                c0, 1024, src_ready=sl1_src_ready,
                                buf_ready=[("pe", p.c["pe"])])
                        sl1_src_ready = None
                        ln_dv1 = p.ln_prev["dv"]
                        ln_dv0 = ln_dv1 - 4
                        p.mb()
                    for half in range(2):
                        hc0 = c0 + half * 512
                        def dma_dq(gp, bi):
                            gp.dma_start(out=wsl[bi % 2].ap()[:, 0:2048].rearrange(
                                "q (k m) -> q k m", m=128), in_=wdq[bi]).then_inc(sems["gq"], 16)
                        emit_proj(
                            p, n_oc=16, wdma=dma_dq, nw=16, wbatch=1,
                            lhsT_view=lambda oc, kc: wsl[oc % 2][:, kc * 128:(kc + 1) * 128],
                            KC=KCE,
                            rhs_view=(lambda half: lambda kc, tc: xb[:, kc * 1024 + half * 512: kc * 1024 + half * 512 + 512])(half),
                            NTC=1,
                            evac=lambda eng, oc, st_, ps: eng.activation(
                                cqc[:, oc * 512:(oc + 1) * 512], ps, AF.Copy),
                            evac_eng="ac",
                            pe_wait0=(lambda v: lambda pe: p.w(pe, "pe", "dv", v))(
                                ln_dv0 if half == 0 else ln_dv1))
                        p.mb()
                        rhs_cqc = lambda kc, tc=0: cqc[:, kc * 512:(kc + 1) * 512]
                        acb_qc = p.c["ac"]
                        sqb_qc = p.c["sq"]
                        qc_prev = list(qc_free)

                        def dma_uq(gp, bi):
                            gp.dma_start(out=wsl[bi % 2].ap()[:, 0:2048].rearrange(
                                "q (k m) -> q k m", m=128), in_=wuq[bi]).then_inc(sems["gq"], 16)

                        def evac_qc(eng, oc, st_, ps):
                            if oc >= 2:
                                p.w(eng, "ac", "sq", sqb_qc + 16 * (oc - 1))
                            else:
                                p.wl(eng, "ac", qc_prev[oc % 2])
                            return eng.activation(
                                qstg2[:, (oc % 2) * 512:(oc % 2) * 512 + 512],
                                ps, AF.Copy)
                        emit_proj(
                            p, n_oc=16, wdma=dma_uq, nw=16, wbatch=1,
                            lhsT_view=lambda oc, kc: wsl[oc % 2][:, kc * 128:(kc + 1) * 128],
                            KC=KCE, rhs_view=rhs_cqc, NTC=1,
                            evac=evac_qc, evac_eng="ac")
                        p.c["sq"] += 16 * 16

                        def qc_out(sy, hc0=hc0, acb_qc=acb_qc):
                            for oc in range(16):
                                p.w(sy, "sy", "ac", acb_qc + oc + 1)
                                sy.dma_start(
                                    out=qc_scr[oc * 128:(oc + 1) * 128, hc0:hc0 + 512],
                                    in_=qstg2[:, (oc % 2) * 512:(oc % 2) * 512 + 512]
                                    ).then_inc(sems["sq"], 16)
                        block.sync(qc_out)
                        qc_free = [[("sq", sqb_qc + 16 * 15)], [("sq", sqb_qc + 16 * 16)]]
                        p.mb()
                        def dma_qr2(gp, j):
                            gp.dma_start(out=wsl[j % 2].ap()[:, 0:2048].rearrange(
                                "q (k m) -> q k m", m=128), in_=wqr2[j]).then_inc(sems["gq"], 16)
                        qr_free = emit_qr_pairs(
                            p, wdma_pair=dma_qr2,
                            lhsT_view=lambda j, kc: wsl[j % 2][:, kc * 128:(kc + 1) * 128],
                            rhs_view=rhs_cqc, W=512, rp2=rp2, qstg=qstg[:, :],
                            qr_scr=qr_scr, c0=hc0,
                            sq_free=qr_free)
                        qr_free = list(qr_free)
                        p.mb()
                    def dma_dkv(gp, bi):
                        gp.dma_start(out=wsl[bi % 2].ap()[:, 0:2048].rearrange(
                            "q (k m) -> q k m", m=128), in_=wdkv[bi]).then_inc(sems["gq"], 16)
                    emit_proj(
                        p, n_oc=4, wdma=dma_dkv, nw=4, wbatch=1,
                        lhsT_view=lambda oc, kc: wsl[oc % 2][:, kc * 128:(kc + 1) * 128],
                        KC=KCE,
                        rhs_view=lambda kc, tc: xb[:, kc * 1024 + tc * 512: kc * 1024 + (tc + 1) * 512],
                        NTC=2,
                        evac=(lambda c0: lambda eng, oc, st_, ps: eng.activation(
                            ckv[:, oc * S + c0: oc * S + c0 + 1024], ps, AF.Copy))(c0),
                        evac_eng="ac")
                    p.mb()
                    kr_wslf = list(p.wsl_free)

                    def dma_kr(gp):
                        p.wl(gp, "gp", kr_wslf[0])
                        gp.dma_start(out=wsl[0].ap()[:, 0:16 * 64].rearrange(
                            "q (k m) -> q k m", m=64),
                            in_=wkr[0]).then_inc(sems["gq"], 16)
                    emit_rope_proj(
                        p, wdma=dma_kr,
                        lhsT_view=lambda kc: wsl[0][:, kc * 64:(kc + 1) * 64],
                        KC=KCE,
                        rhs_view=lambda kc, tc: xb[:, kc * 1024 + tc * 512: kc * 1024 + (tc + 1) * 512],
                        dest=kR, rp=rp, NTC=2, tok0=c0)
                    p.wsl_free = [[("pe", p.c["pe"])], [("pe", p.c["pe"])]]
                    p.mb()
                if True:
                    p.tap("ckv", ckv[:, :], [128, 4 * S], BF16)
                    p.tap("kr", kR[0:64, :], [64, S], BF16)
                    p.tap("qcs", qc_scr, [E, S], BF16)
                    p.tap("qrs", qr_scr, [1024, S], BF16)
                    p.ck("p2")

                with contextlib.ExitStack() as ph:
                    ten = {
                        "qch2": [ph.enter_context(sb(f"qch{i}", [128, S], BF16)) for i in range(2)],
                        "qrh2": [ph.enter_context(sb(f"qrh{i}", [64, S], BF16)) for i in range(2)],
                        "kh": ph.enter_context(sb("kh", [128, S], BF16)),
                        "vtm": ph.enter_context(sb("vtm", [128, 16 * 128], BF16)),
                        "probs": ph.enter_context(sb("probs", [128, 16 * 512], BF16)),
                        "wuk_sb": [ph.enter_context(sb(f"wuks{i}", [128, 4 * 128], BF16)) for i in range(2)],
                        "wuv_sb": [ph.enter_context(sb(f"wuvs{i}", [128, 4 * 128], BF16)) for i in range(2)],
                        "aosl": ph.enter_context(sb("aosl", [128, 2 * 512], BF16)),
                        "ckv": ckv[:, :], "kR": kR,
                        "qc_scr": qc_scr, "qr_scr": qr_scr,
                        "wuk": wuk, "wuvr": wuvr,
                        "ones_b": ones_b[:, :], "ones_fr": ones_fr[:, :],
                        "c_negB": c_negB[:, :], "c_zero": c_zero[0:1, 0:1],
                        "rs_row": rs_row, "rs_sb": rs_sb[:, :],
                        "maska_sb": maska_sb[:, :], "ao_scr": ao_scr,
                    }
                    hb = {"wfree": p.c["pe"], "qfree": p.c["pe"]}
                    for h in range(H):
                        emit_attn_a_head(p, h, hb, ten)
                        if h == 0 and _TAPS:
                            p.barrier()
                            p.tap("qch0", ten["qch2"][0][:, :], [128, S], BF16)
                            p.tap("qrh0", ten["qrh2"][0][0:64, :], [64, S], BF16)
                            p.tap("kh0", ten["kh"][:, :], [128, S], BF16)
                            p.tap("vtm0", ten["vtm"][:, :], [128, 16 * 128], BF16)
                        if h == 0:
                            p.ck("p3h0")
                        if h == 3:
                            p.ck("p3h3")
                    p.bs_owner = [[("ac", p.c["ac"]), ("dv", p.c["dv"])],
                                  [("ac", p.c["ac"]), ("dv", p.c["dv"])]]
                    p.wsl_free = [[("pe", p.c["pe"])], [("pe", p.c["pe"])]]
                    p3_done = {"pe": p.c["pe"], "dv": p.c["dv"], "sq": p.c["sq"]}
                    p.mb()
                p.tap("ao", ao_scr, [E, S], BF16)
                p.ck("p3")
                qstk.close()

                with contextlib.ExitStack() as ph:
                    ao_sb = ph.enter_context(sb("ao_sb", [128, KCE * S], BF16))
                    of32l = ph.enter_context(sb("of32a", [128, 2 * 1024], F32))
                    sqb0 = p.c["sq"]
                    p.c["sq"] += 16 * KCE

                    def ao_ld(sy):
                        p.w(sy, "sy", "pe", p3_done["pe"])
                        p.w(sy, "sy", "dv", p3_done["dv"])
                        p.w(sy, "sy", "sq", p3_done["sq"])
                        for kc in range(KCE):
                            sy.dma_start(out=ao_sb[:, kc * S:(kc + 1) * S],
                                         in_=ao_scr[kc * 128:(kc + 1) * 128, :]).then_inc(sems["sq"], 16)
                    block.sync(ao_ld)

                    def dma_wo(gp, bi):
                        gp.dma_start(out=wsl[bi % 2].ap()[:, 0:2048].rearrange(
                            "q (k m) -> q k m", m=128), in_=wo[bi]).then_inc(sems["gq"], 16)
                    emit_proj_accum(
                        p, n_oc=16, wdma=dma_wo, nw=16, wbatch=1,
                        lhsT_view=lambda oc, kc: wsl[oc % 2][:, kc * 128:(kc + 1) * 128],
                        KC=KCE,
                        rhs_view=lambda kc, tc: ao_sb[:, kc * S + tc * 512: kc * S + (tc + 1) * 512],
                        NTC=4,
                        dest_piece=lambda oc, pc: Xs[oc * 128:(oc + 1) * 128, pc * 1024:(pc + 1) * 1024],
                        of32=of32l[:, :],
                        pe_wait0=lambda pe: p.w(pe, "pe", "sq", sqb0 + 16 * KCE))
                    p.mb()
            p.tap("X1", Xs, [E, S], F32)
            p.ck("a2")

            ffn_app(sq, Xs, src_ready=[("gq", p.c["gq"])])
            p.tap("X2", Xs, [E, S], F32)
            p.ck("sl2")

            sl3_src_ready = [("gq", p.c["gq"])]
            for g in range(2):
                with contextlib.ExitStack() as gstk:
                    vtm_g = gstk.enter_context(sb("vtm_g", [128, 16 * 8 * 128], BF16))
                    Qg = gstk.enter_context(sb("Qg", [128, 8 * S], BF16))
                    Kg = gstk.enter_context(sb("Kg", [128, 8 * S], BF16))
                    for ch in range(2):
                      c0 = ch * 1024
                      with contextlib.ExitStack() as ph:
                        xb = ph.enter_context(sb("xb3", [128, KCE * 1024], BF16,
                                                 side="right"))
                        with contextlib.ExitStack() as lns:
                            t = alloc_ln(lns, nq=1)
                            emit_ln(p, t, Xs,
                                    lambda kc, cc, w: xb[:, kc * 1024 + cc: kc * 1024 + cc + w],
                                    lambda cc, w: xb.ap().rearrange("q (kc n) -> q kc n", n=1024)[:, :, cc:cc + w],
                                    c0, 1024,
                                    src_ready=(sl3_src_ready
                                               if sl3_src_ready is not None
                                               else [("pe", p.c["pe"]),
                                                     ("gq", p.c["gq"]),
                                                     ("dv", p.c["dv"])]),
                                    buf_ready=[("pe", p.c["pe"]),
                                               ("sq", p.c["sq"])])
                            sl3_src_ready = None
                            ln_dv = p.ln_prev["dv"]
                            p.mb()
                        with contextlib.ExitStack() as ws:
                            wvh = ws.enter_context(sb("wvh", [128, 16 * 512], BF16))
                            prev_pe = p.c["pe"]
                            bso_v = [list(p.bs_owner[0] or []), list(p.bs_owner[1] or [])]
                            for half in range(2):
                                gqb0 = p.c["gq"]
                                p.c["gq"] += 16
                                peb0, acb0 = p.c["pe"], p.c["ac"]
                                p.c["pe"] += 8
                                p.c["ac"] += 8

                                def wv_ld(gp, half=half, gqb0=gqb0, prev_pe=prev_pe):
                                    p.w(gp, "gp", "pe", prev_pe)
                                    p.w(gp, "gp", "dv", ln_dv)
                                    gp.dma_start(
                                        out=wvh.ap().rearrange("q (k m) -> q k m", m=512),
                                        in_=wvr[g].rearrange("k q m -> q k m")[:, :, half * 512:(half + 1) * 512]
                                    ).then_inc(sems["gq"], 16)
                                block.gpsimd(wv_ld)

                                def pe_f(pe, ch=ch, half=half, gqb0=gqb0, peb0=peb0, acb0=acb0):
                                    p.w(pe, "pe", "gq", gqb0 + 16)
                                    p.w(pe, "pe", "ac", acb0)
                                    if half == 0:
                                        p.wl(pe, "pe", bso_v[0])
                                        p.wl(pe, "pe", bso_v[1])
                                        p.w(pe, "pe", "dv", ln_dv)
                                    for mm in range(8):
                                        m = 8 * ch + mm
                                        ins = None
                                        for kc in range(KCE):
                                            ins = pe.matmul(
                                                bk(p.PS, mm),
                                                xb[:, kc * 1024 + mm * 128: kc * 1024 + (mm + 1) * 128],
                                                wvh[:, kc * 512:(kc + 1) * 512],
                                                start=(kc == 0), stop=(kc == KCE - 1))
                                        ins.then_inc(p.sems["pe"], 1)
                                block.tensor(pe_f)

                                def ac_f(ac, ch=ch, half=half, peb0=peb0):
                                    for mm in range(8):
                                        m = 8 * ch + mm
                                        p.w(ac, "ac", "pe", peb0 + mm + 1)
                                        ac.activation(
                                            vtm_g[:, m * 1024 + half * 512: m * 1024 + half * 512 + 512],
                                            bk(p.PS, mm),
                                            AF.Copy).then_inc(p.sems["ac"], 1)
                                block.scalar(ac_f)
                                prev_pe = peb0 + 8
                            p.bs_owner = [[("ac", p.c["ac"])], [("ac", p.c["ac"])]]
                            p.mb()

                        for wsrc, dst in [(wq, Qg), (wk, Kg)]:
                            def dma_w(gp, bi, wsrc=wsrc):
                                gp.dma_start(out=wsl[bi % 2].ap()[:, 0:2048].rearrange(
                                    "q (k m) -> q k m", m=128),
                                    in_=wsrc[g * 8 + bi]).then_inc(sems["gq"], 16)
                            emit_proj(
                                p, n_oc=8, wdma=dma_w, nw=8, wbatch=1,
                                lhsT_view=lambda oc, kc: wsl[oc % 2][:, kc * 128:(kc + 1) * 128],
                                KC=KCE,
                                rhs_view=lambda kc, tc: xb[:, kc * 1024 + tc * 512: kc * 1024 + (tc + 1) * 512],
                                NTC=2,
                                evac=(lambda dst, c0: lambda eng, oc, st_, ps: eng.activation(
                                    dst[:, oc * S + c0: oc * S + c0 + 1024], ps, AF.Copy))(dst, c0),
                                evac_eng="ac")
                            p.mb()

                    ao_g = gstk.enter_context(sb("ao_g", [128, 8 * S], BF16,
                                                 side="right"))
                    probs3 = gstk.enter_context(sb("probs3", [128, 16 * 128], BF16))
                    ten3 = {
                        "Qg": Qg[:, :], "Kg": Kg[:, :], "vtm_g": vtm_g[:, :],
                        "ao_g": ao_g[:, :], "probs": probs3[:, :],
                        "ones_b": ones_b[:, :], "ones_fr": ones_fr[:, :],
                        "c_negB": c_negB[:, :], "c_zero": c_zero[0:1, 0:1],
                        "rs_row": rs_row, "rs_sb": rs_sb[:, :],
                        "maskb_sb": maskb_sb[:, :],
                    }
                    emit_attn_b_group(p, ten3)
                    p.bs_owner = [[("ac", p.c["ac"]), ("dv", p.c["dv"])],
                                  [("ac", p.c["ac"]), ("dv", p.c["dv"])]]
                    attnb_dv = p.c["dv"]
                    p.mb()
                    if g == 0:
                        p.tap("aog0", ao_g[:, :], [128, 8 * S], BF16)
                    p.c["sq"] += 16 * 8

                    def ao_dump(sy, g=g, ao_ap=ao_g[:, :], attnb_dv=attnb_dv):
                        p.w(sy, "sy", "dv", attnb_dv)
                        for hl in range(8):
                            sy.dma_start(
                                out=ao_scr[(g * 8 + hl) * 128:(g * 8 + hl + 1) * 128, :],
                                in_=ao_ap[:, hl * S:(hl + 1) * S]).then_inc(sems["sq"], 16)
                    block.sync(ao_dump)
                    p.mb()
                    p.ck(f"sl3g{g}")

            with contextlib.ExitStack() as ph:
                ao_sb = ph.enter_context(sb("ao_sb3", [128, KCE * S], BF16))
                of32w = ph.enter_context(sb("of32w", [128, 2 * 1024], F32))
                sqb0 = p.c["sq"]
                sl3_done = {"pe": p.c["pe"], "dv": p.c["dv"]}
                p.c["sq"] += 16 * KCE

                def ao_ld(sy):
                    p.w(sy, "sy", "pe", sl3_done["pe"])
                    p.w(sy, "sy", "dv", sl3_done["dv"])
                    p.w(sy, "sy", "sq", sqb0)
                    for kc in range(KCE):
                        sy.dma_start(out=ao_sb[:, kc * S:(kc + 1) * S],
                                     in_=ao_scr[kc * 128:(kc + 1) * 128, :]).then_inc(sems["sq"], 16)
                block.sync(ao_ld)

                def dma_woo(gp, bi):
                    gp.dma_start(out=wsl[bi % 2].ap()[:, 0:2048].rearrange(
                        "q (k m) -> q k m", m=128), in_=woo[bi]).then_inc(sems["gq"], 16)
                emit_proj_accum(
                    p, n_oc=16, wdma=dma_woo, nw=16, wbatch=1,
                    lhsT_view=lambda oc, kc: wsl[oc % 2][:, kc * 128:(kc + 1) * 128],
                    KC=KCE,
                    rhs_view=lambda kc, tc: ao_sb[:, kc * S + tc * 512: kc * S + (tc + 1) * 512],
                    NTC=4,
                    dest_piece=lambda oc, pc: Xs[oc * 128:(oc + 1) * 128, pc * 1024:(pc + 1) * 1024],
                    of32=of32w[:, :],
                    pe_wait0=lambda pe: p.w(pe, "pe", "sq", sqb0 + 16 * KCE))
                p.mb()
            p.tap("X3", Xs, [E, S], F32)
            p.ck("sl3")

            p.c["gq"] += 16

            def ycopy(gp):
                gp.dma_start(out=y_out[sq], in_=Xs).then_inc(sems["gq"], 16)
            block.gpsimd(ycopy)
            p.mb()
            ffn_app(sq, y_out[sq], src_ready=[("gq", p.c["gq"])])
            p.ck("sl4")

        except _StopEmission:
            pass
        snap = dict(p.c)

        def fin(sy):
            for s, v in snap.items():
                p.w(sy, "sy", s, v)
        block.sync(fin)
        block.__exit__(None, None, None)
        nc._top_stack = top

    return nc, p


def kernel(**inp):
    nc, p = build_nc()
    kernel.mb_sites = getattr(p, "mb_sites", [])
    arrs = prepare_weights(inp)
    x = inp["L0_attn_a_x"]
    in_maps = []
    for c in range(8):
        m = dict(arrs)
        m["x"] = np.ascontiguousarray(
            x[2 * c:2 * c + 2].transpose(0, 2, 1)).astype(np.float32, copy=False)
        in_maps.append(m)
    _tr = bool(int(_os.environ.get("ATHENA_TRACE", "0")))
    res = run_bass_kernel_spmd(nc, in_maps, core_ids=list(range(8)),
                               trace=_tr, trace_cores=[0] if _tr else None)
    out = np.empty((16, 2048, 2048), np.float32)
    for c in range(8):
        y = np.asarray(res.results[c]["y"])
        out[2 * c:2 * c + 2] = y.transpose(0, 2, 1)
    if _TAPS:
        kernel.taps = [{n: np.asarray(res.results[c]["tap_" + n]) for n in p.tap_names}
                       for c in range(8)]
    return out


# version 03 (final)
# speedup vs baseline: 1.1641x; 1.1641x over previous
import math
import contextlib
import numpy as np
import ml_dtypes

import concourse.bass as bass
import concourse.mybir as mybir
from concourse.bass_utils import run_bass_kernel_spmd

F32 = mybir.dt.float32
BF16 = mybir.dt.bfloat16
AF = mybir.ActivationFunctionType
ALU = mybir.AluOpType

S = 2048
E = 2048
H = 16
DH = 128
LOW = 512
R = 64
KCE = E // 128
SCALE_A = 1.0 / math.sqrt(DH + R)
SCALE_B = DH ** (-0.5)
NEG_EXP_BIAS = -12.0
EPS = 1e-5
NSEQ = 2

_TAPS = set()
_STOP_AFTER = None
import os as _os
_SAFE = bool(int(_os.environ.get("ATHENA_SAFE", "0")))
def _parse_mb(spec):
    out = set()
    for part in spec.split(","):
        part = part.strip()
        if not part:
            continue
        if "-" in part:
            a, b = part.split("-")
            out.update(range(int(a), int(b) + 1))
        else:
            out.add(int(part))
    return out


_MB_ON = _parse_mb(_os.environ.get("ATHENA_MB", ""))
_PB = {}
for _part in _os.environ.get("ATHENA_PB", "").split(","):
    if ":" in _part:
        _f = _part.split(":")
        _PB[int(_f[0])] = (_f[1].split("+"), _f[2].split("+") if len(_f) > 2 else None)


class _StopEmission(Exception):
    pass


BF = ml_dtypes.bfloat16



def lhsT_layout(W):
    K, O = W.shape
    Kc = K // 128
    ow = 128 if O >= 128 else O
    Oc = O // ow
    return np.ascontiguousarray(W.reshape(Kc, 128, Oc, ow).transpose(2, 1, 0, 3))


_ROPE_PERM = np.concatenate([np.arange(0, R, 2), np.arange(1, R, 2)])


def build_rope_tables():
    half = R // 2
    freqs = 10000.0 ** (-np.arange(half, dtype=np.float64) / half)
    theta = np.outer(freqs, np.arange(S, dtype=np.float64))
    cos = np.cos(theta).astype(np.float32)
    sin = np.sin(theta).astype(np.float32)
    csk = np.concatenate([cos, sin], 0)
    sck = np.concatenate([sin, cos], 0)
    r1 = np.zeros((64, 64), np.float32)
    r1[np.arange(32), np.arange(32)] = 1.0
    r1[np.arange(32) + 32, np.arange(32)] = -1.0
    r2 = np.zeros((64, 64), np.float32)
    r2[np.arange(32), np.arange(32) + 32] = 1.0
    r2[np.arange(32) + 32, np.arange(32) + 32] = 1.0
    rmix = np.concatenate([r1, r2], 1).astype(BF)
    return csk, sck, rmix


def build_maska():
    i = np.arange(128)[:, None]
    j = np.arange(512)[None, :]
    return np.ascontiguousarray(
        np.stack([(j >= d * 128 + i) for d in range(4)]).astype(BF))


TLW = [list(range(16)), [0] + list(range(2, 10)),
       [0] + list(range(6, 14)), [0] + list(range(10, 16))]


def maskw_idx(qb, t):
    if qb == 0:
        return 9 + t if t < 6 else 15
    return 8 if t == 0 else t - (4 * qb - 2)


def build_maskw():
    i = np.arange(128)[:, None]
    j = np.arange(512)[None, :]
    tiles = [np.abs((d - 2) * 128 + i - j) < 256 for d in range(8)]
    tiles.append((i < 16) & np.ones((1, 512), bool))
    tiles += [(np.abs(t * 128 + i - j) < 256) | (t * 128 + i < 16) | (j < 16)
              for t in range(6)]
    tiles.append((j < 16) & np.ones((128, 1), bool))
    return np.ascontiguousarray(np.stack(tiles).astype(BF))


def build_maskb():
    i = np.arange(128)[:, None]
    j = np.arange(128)[None, :]
    pats, masks, tiles = {}, [], {}
    for qb in range(16):
        lst = []
        for tk in range(16):
            k = tk * 128 + i
            q = qb * 128 + j
            v = (np.abs(k - q) < 256) | (k < 16) | (q < 16)
            if not v.any():
                continue
            if v.all():
                lst.append((tk, None))
            else:
                key = v.tobytes()
                if key not in pats:
                    pats[key] = len(masks)
                    masks.append(v.astype(BF))
                lst.append((tk, pats[key]))
        tiles[qb] = lst
    return tiles, np.ascontiguousarray(np.stack(masks))


MASKB_TILES, MASKB_ARR = build_maskb()
NMASKB = MASKB_ARR.shape[0]


def prepare_weights(inp):
    d = {}
    d["wdq"] = lhsT_layout(inp["L0_attn_a_W_DQ"])
    d["wdkv"] = lhsT_layout(inp["L0_attn_a_W_DKV"])
    d["wkr"] = lhsT_layout(np.ascontiguousarray(inp["L0_attn_a_W_KR"][:, _ROPE_PERM]))
    d["wuq"] = lhsT_layout(inp["L0_attn_a_W_UQ"])
    wqr = inp["L0_attn_a_W_QR"].reshape(E, H, R)[:, :, _ROPE_PERM]
    d["wqr2"] = np.ascontiguousarray(
        wqr.reshape(KCE, 128, 8, 128).transpose(2, 1, 0, 3))
    d["wuk"] = lhsT_layout(inp["L0_attn_a_W_UK"])
    d["wuvr"] = np.ascontiguousarray(
        inp["L0_attn_a_W_UV"].reshape(4, 128, H, 128).transpose(2, 0, 1, 3))
    d["wo"] = lhsT_layout(inp["L0_attn_a_W_O"])
    d["wq"] = lhsT_layout(inp["L0_attn_b_Wq"])
    d["wk"] = lhsT_layout(inp["L0_attn_b_Wk"])
    d["wvr"] = np.ascontiguousarray(
        inp["L0_attn_b_Wv"].reshape(KCE, 128, 2, 1024).transpose(2, 0, 1, 3))
    d["woo"] = lhsT_layout(inp["L0_attn_b_Wo"])
    d["w1"] = lhsT_layout(np.ascontiguousarray(inp["L0_ffn_W1"].T))
    d["w2"] = lhsT_layout(np.ascontiguousarray(inp["L0_ffn_W2"].T))
    d["w3"] = lhsT_layout(np.ascontiguousarray(inp["L0_ffn_W3"].T))
    d["b1c"] = np.ascontiguousarray(inp["L0_ffn_b1"].reshape(32, 128).T).astype(np.float32)
    d["b2c"] = np.ascontiguousarray(inp["L0_ffn_b2"].reshape(32, 128).T).astype(np.float32)
    d["b3r"] = np.ascontiguousarray(inp["L0_ffn_b3"].reshape(1, E)).astype(np.float32)
    csk, sck, rmix = build_rope_tables()
    d["csk2"] = np.ascontiguousarray(np.concatenate([csk, csk], 0))
    d["sck2"] = np.ascontiguousarray(np.concatenate([sck, sck], 0))
    z = np.zeros((64, 64), np.float32)
    r1 = rmix[:, 0:64].astype(np.float32)
    r2 = rmix[:, 64:128].astype(np.float32)
    d["rmix2"] = np.ascontiguousarray(np.concatenate([
        np.block([[r1, z], [z, r1]]), np.block([[r2, z], [z, r2]])],
        1)).astype(BF)
    d["rmix"] = rmix
    d["maska"] = build_maska()
    d["maskw"] = build_maskw()
    return d



_WLOG = []
_ALLOCLOG = []


class P:
    def __init__(self, nc, block, sems):
        self.nc = nc
        self.block = block
        self.sems = sems
        self.c = {k: 0 for k in sems}
        self.hw = {}
        self.tap_names = []
        self.wsl_free = [None, None]
        self.bs_owner = [None, None]
        self.of32_free = [None, None]
        self.ln_prev = None

    def w(self, eng, ename, sem, val):
        if val is None or val <= 0:
            return
        key = (ename, sem)
        if self.hw.get(key, 0) >= val:
            _WLOG.append((ename, "wait-ELIDED", sem, val))
            return
        _WLOG.append((ename, "wait", sem, val))
        eng.wait_ge(self.sems[sem], val)
        self.hw[key] = val

    def wl(self, eng, ename, lst):
        if lst:
            for sem, val in lst:
                self.w(eng, ename, sem, val)

    def barrier(self):
        snap = dict(self.c)

        def mk(ename):
            def f(eng):
                for s, v in snap.items():
                    self.w(eng, ename, s, v)
            return f
        self.block.tensor(mk("pe"))
        self.block.vector(mk("dv"))
        self.block.scalar(mk("ac"))
        self.block.gpsimd(mk("gp"))
        self.block.sync(mk("sy"))
        self.wsl_free = [None, None]
        self.bs_owner = [None, None]
        self.of32_free = [None, None]
        self.ln_prev = None

    def mb(self):
        self._mb_i = getattr(self, "_mb_i", 0) + 1
        import inspect
        ln = inspect.currentframe().f_back.f_lineno
        if not hasattr(self, "mb_sites"):
            self.mb_sites = []
        self.mb_sites.append((self._mb_i, ln))
        if _SAFE or self._mb_i in _MB_ON:
            self.barrier()
        elif self._mb_i in _PB:
            snap = dict(self.c)
            regs = {"pe": self.block.tensor, "dv": self.block.vector,
                    "ac": self.block.scalar, "gp": self.block.gpsimd,
                    "sy": self.block.sync}

            engs, semsel = _PB[self._mb_i]

            def mkp(ename):
                def f(eng):
                    for s_, v in snap.items():
                        if semsel is None or s_ in semsel:
                            self.w(eng, ename, s_, v)
                return f
            for ename in engs:
                regs[ename](mkp(ename))

    def ck(self, tag):
        if _STOP_AFTER == tag:
            raise _StopEmission

    def tap(self, name, src_ap, shape, dtype):
        if name not in _TAPS or name in self.tap_names:
            return
        t = self.nc.dram_tensor("tap_" + name, list(shape), dtype,
                                kind="ExternalOutput").ap()
        self.tap_names.append(name)
        self.c["sq"] += 16
        snap = dict(self.c)

        def f(sy):
            for s, v in snap.items():
                if s != "sq":
                    self.w(sy, "sy", s, v)
            sy.dma_start(out=t, in_=src_ap).then_inc(self.sems["sq"], 16)
            self.w(sy, "sy", "sq", snap["sq"])
        self.block.sync(f)
        self.barrier()


def bk(PS, i, w=512):
    return PS[:, i * 512:i * 512 + w]


def emit_proj(p, *, n_oc, wdma, nw, wbatch, lhsT_view, KC, rhs_view, NTC,
              evac, evac_eng="ac", rank1=None, pe_wait0=None, mpart=128):
    PS = p.PS
    nsets = 8 // NTC
    gqb, peb, evb = p.c["gq"], p.c["pe"], p.c[evac_eng]
    p.c["gq"] += 16 * nw
    p.c["pe"] += n_oc
    p.c[evac_eng] += n_oc

    wslf = list(p.wsl_free)
    bso = list(p.bs_owner)

    def gp_f(gp):
        for bi in range(nw):
            if bi >= 2:
                p.w(gp, "gp", "pe", peb + (bi - 1) * wbatch)
            else:
                p.wl(gp, "gp", wslf[bi % 2])
            wdma(gp, bi)
    p.block.gpsimd(gp_f)

    def pe_f(pe):
        if pe_wait0 is not None:
            pe_wait0(pe)
        for i in range(n_oc):
            p.w(pe, "pe", "gq", gqb + 16 * (i // wbatch + 1))
            st = i % nsets
            if i >= nsets:
                p.w(pe, "pe", evac_eng, evb + i - nsets + 1)
            else:
                p.wl(pe, "pe", bso[(st * NTC) // 4])
            first = True
            if rank1 is not None:
                for tc in range(NTC):
                    rank1(pe, i, tc, bk(PS, st * NTC + tc)[:mpart])
                first = False
            ins = None
            for kc in range(KC):
                lt = lhsT_view(i, kc)
                for tc in range(NTC):
                    ins = pe.matmul(bk(PS, st * NTC + tc)[:mpart], lt,
                                    rhs_view(kc, tc),
                                    start=(first and kc == 0),
                                    stop=(kc == KC - 1))
            ins.then_inc(p.sems["pe"], 1)
    p.block.tensor(pe_f)

    def ev_f(eng):
        for i in range(n_oc):
            p.w(eng, evac_eng, "pe", peb + i + 1)
            st = i % nsets
            ps = PS[:mpart, st * NTC * 512:(st * NTC + NTC) * 512]
            evac(eng, i, st, ps).then_inc(p.sems[evac_eng], 1)
    {"ac": p.block.scalar, "dv": p.block.vector}[evac_eng](ev_f)

    p.wsl_free = [[("pe", peb + n_oc)], [("pe", peb + n_oc)]]
    p.bs_owner = [[(evac_eng, evb + n_oc)], [(evac_eng, evb + n_oc)]]


def emit_proj_accum(p, *, n_oc, wdma, nw, wbatch, lhsT_view, KC, rhs_view, NTC,
                    dest_piece, of32, rank1=None, pe_wait0=None):
    PS = p.PS
    nsets = 8 // NTC
    npc = NTC // 2
    gqb, peb, acb = p.c["gq"], p.c["pe"], p.c["ac"]
    p.c["pe"] += n_oc
    p.c["ac"] += npc * n_oc

    gq_w = [0] * nw
    gq_acc = [[0] * npc for _ in range(n_oc)]
    cnt = 0
    order = []
    for i in range(n_oc + 2):
        if i < nw:
            cnt += 1
            gq_w[i] = gqb + 16 * cnt
            order.append(("w", i))
        if 2 <= i < n_oc + 2:
            for pc in range(npc):
                cnt += 1
                gq_acc[i - 2][pc] = gqb + 16 * cnt
                order.append(("a", i - 2, pc))
    p.c["gq"] = gqb + 16 * cnt

    wslf = list(p.wsl_free)
    bso = list(p.bs_owner)
    of32f = list(p.of32_free)

    def gp_f(gp):
        for ev in order:
            if ev[0] == "w":
                bi = ev[1]
                if bi >= 2:
                    p.w(gp, "gp", "pe", peb + (bi - 1) * wbatch)
                else:
                    p.wl(gp, "gp", wslf[bi % 2])
                wdma(gp, bi)
            else:
                _, oc, pc = ev
                j = oc * npc + pc
                p.w(gp, "gp", "ac", acb + j + 1)
                gp.dma_start(out=dest_piece(oc, pc),
                             in_=of32[:, (j % 2) * 1024:(j % 2) * 1024 + 1024],
                             accum_op=ALU.add).then_inc(p.sems["gq"], 16)
        p.w(gp, "gp", "gq", p.c["gq"])
    p.block.gpsimd(gp_f)

    def pe_f(pe):
        if pe_wait0 is not None:
            pe_wait0(pe)
        for i in range(n_oc):
            p.w(pe, "pe", "gq", gq_w[i // wbatch])
            st = i % nsets
            if i >= nsets:
                p.w(pe, "pe", "ac", acb + npc * (i - nsets + 1))
            else:
                p.wl(pe, "pe", bso[(st * NTC) // 4])
            first = True
            if rank1 is not None:
                for tc in range(NTC):
                    rank1(pe, i, tc, bk(PS, st * NTC + tc))
                first = False
            ins = None
            for kc in range(KC):
                lt = lhsT_view(i, kc)
                for tc in range(NTC):
                    ins = pe.matmul(bk(PS, st * NTC + tc), lt, rhs_view(kc, tc),
                                    start=(first and kc == 0),
                                    stop=(kc == KC - 1))
            ins.then_inc(p.sems["pe"], 1)
    p.block.tensor(pe_f)

    def ac_f(ac):
        for i in range(n_oc):
            p.w(ac, "ac", "pe", peb + i + 1)
            st = i % nsets
            for pc in range(npc):
                j = i * npc + pc
                if j >= 2:
                    jj = j - 2
                    p.w(ac, "ac", "gq", gq_acc[jj // npc][jj % npc])
                else:
                    p.wl(ac, "ac", of32f[j % 2])
                ac.activation(of32[:, (j % 2) * 1024:(j % 2) * 1024 + 1024],
                              PS[:, (st * NTC + 2 * pc) * 512:(st * NTC + 2 * pc + 2) * 512],
                              AF.Copy).then_inc(p.sems["ac"], 1)
    p.block.scalar(ac_f)

    p.wsl_free = [[("pe", peb + n_oc)], [("pe", peb + n_oc)]]
    p.bs_owner = [[("ac", acb + npc * n_oc)], [("ac", acb + npc * n_oc)]]
    p.of32_free = [[("gq", p.c["gq"])], [("gq", p.c["gq"])]]


def emit_ln(p, t, src_hbm, xb_view, xb_ap3, tok0, ntok, src_ready=None,
            buf_ready=None, SC=256):
    PS = p.PS
    nsc = ntok // SC
    m_row = t["m_row"][0:1, 0:SC]
    s2_row = t["s2_row"][0:1, 0:SC]
    var_row = t["var_row"][0:1, 0:SC]
    r_row = t["r_row"][0:1, 0:SC]
    sqb, gqb = p.c["sq"], p.c["gq"]
    peb, acb, dvb, pob = p.c["pe"], p.c["ac"], p.c["dv"], p.c["po"]
    p.c["sq"] += 16 * nsc
    p.c["gq"] += 16 * nsc
    p.c["pe"] += 4 * nsc
    p.c["ac"] += 2 * nsc
    p.c["dv"] += 2 * nsc
    p.c["po"] += nsc
    prev = p.ln_prev or {"pe": 0, "ac": 0, "dv": 0, "po": 0}
    bso1 = list(p.bs_owner[1] or [])

    src3 = src_hbm.rearrange("(kc q) n -> q kc n", q=128)

    def sy_f(sy):
        p.wl(sy, "sy", src_ready)
        for sc in range(nsc):
            p.w(sy, "sy", "sq", sqb + 16 * sc)
            if sc >= 2:
                p.w(sy, "sy", "dv", dvb + 2 * (sc - 1))
            else:
                p.w(sy, "sy", "dv", prev["dv"])
            c0 = tok0 + sc * SC
            _WLOG.append(("sy", "dma-lnload", tok0, sc))
            sy.dma_start(
                out=t["xf32"][sc % 2].ap().rearrange("q (kc n) -> q kc n", n=SC),
                in_=src3[:, :, c0:c0 + SC]).then_inc(p.sems["sq"], 16)
    p.block.sync(sy_f)

    def gp_f(gp):
        p.wl(gp, "gp", src_ready)
        p.wl(gp, "gp", buf_ready)
        for sc in range(nsc):
            p.w(gp, "gp", "gq", gqb + 16 * sc)
            c0 = tok0 + sc * SC
            gp.dma_start(
                out=xb_ap3(sc * SC, SC),
                in_=src3[:, :, c0:c0 + SC]).then_inc(p.sems["gq"], 16)
        p.w(gp, "gp", "gq", gqb + 16 * nsc)
    p.block.gpsimd(gp_f)

    def pe_f(pe):
        for sc in range(nsc):
            p.w(pe, "pe", "gq", gqb + 16 * (sc + 1))
            if sc >= 1:
                p.w(pe, "pe", "ac", acb + 2 * (sc - 1) + 1)
            else:
                p.wl(pe, "pe", bso1)
                p.wl(pe, "pe", src_ready)
                p.w(pe, "pe", "ac", prev["ac"])
                p.w(pe, "pe", "dv", prev["dv"])
            ins = None
            for kc in range(KCE):
                ins = pe.matmul(bk(PS, 4)[0:1, :SC], t["ones_b"],
                                xb_view(kc, sc * SC, SC),
                                start=(kc == 0), stop=(kc == KCE - 1))
            ins.then_inc(p.sems["pe"], 1)
            p.w(pe, "pe", "po", pob + sc + 1)
            for kc in range(KCE):
                ins = pe.matmul(bk(PS, 5)[0:1, :SC], t["ones_b"],
                                t["xsq"][sc % len(t["xsq"])][:, kc * SC:(kc + 1) * SC],
                                start=(kc == 0), stop=(kc == KCE - 1))
            ins.then_inc(p.sems["pe"], 1)
            p.w(pe, "pe", "ac", acb + 2 * sc + 2)
            if sc >= 1:
                p.w(pe, "pe", "dv", dvb + 2 * sc)
            pe.matmul(bk(PS, 6)[:, :SC], t["ones_fr"], m_row,
                      start=True, stop=True).then_inc(p.sems["pe"], 1)
            pe.matmul(bk(PS, 7)[:, :SC], t["ones_fr"], r_row,
                      start=True, stop=True).then_inc(p.sems["pe"], 1)
    p.block.tensor(pe_f)

    def ac_f(ac):
        for sc in range(nsc):
            par = sc % 2
            p.w(ac, "ac", "gq", gqb + 16 * (sc + 1))
            nq = len(t["xsq"])
            if sc >= nq:
                p.w(ac, "ac", "pe", peb + 4 * (sc - nq) + 2)
            else:
                p.w(ac, "ac", "pe", prev["pe"])
                p.wl(ac, "ac", src_ready)
            ac.activation(t["xsq"][sc % len(t["xsq"])][:, :], xb_ap3(sc * SC, SC),
                          AF.Square).then_inc(p.sems["po"], 1)
            p.w(ac, "ac", "pe", peb + 4 * sc + 2)
            if sc >= 1:
                p.w(ac, "ac", "dv", dvb + 2 * (sc - 1) + 1)
                p.w(ac, "ac", "pe", peb + 4 * sc)
            else:
                p.w(ac, "ac", "dv", prev["dv"])
            ac.activation(m_row, bk(PS, 4)[0:1, :SC], AF.Copy, scale=1.0 / E)
            ac.activation(s2_row, bk(PS, 5)[0:1, :SC], AF.Copy,
                          scale=1.0 / E).then_inc(p.sems["ac"], 1)
            p.w(ac, "ac", "dv", dvb + 2 * sc + 1)
            ac.activation(r_row, var_row, AF.Ln, bias=t["c_eps"])
            ac.activation(r_row, r_row, AF.Exp,
                          scale=-0.5).then_inc(p.sems["ac"], 1)
    p.block.scalar(ac_f)

    def dv_f(dv):
        for sc in range(nsc):
            par = sc % 2
            p.w(dv, "dv", "ac", acb + 2 * sc + 1)
            dv.tensor_tensor(var_row, m_row, m_row, ALU.mult)
            dv.tensor_tensor(var_row, s2_row, var_row,
                             ALU.subtract).then_inc(p.sems["dv"], 1)
            p.w(dv, "dv", "pe", peb + 4 * sc + 4)
            p.w(dv, "dv", "sq", sqb + 16 * (sc + 1))
            ins = None
            _WLOG.append(("dv", "dma-lnnorm", tok0, sc))
            for kc in range(KCE):
                dv.tensor_tensor(t["ts"][:, :SC],
                                 t["xf32"][par][:, kc * SC:(kc + 1) * SC],
                                 bk(PS, 6)[:, :SC], ALU.subtract)
                ins = dv.tensor_tensor(xb_view(kc, sc * SC, SC), t["ts"][:, :SC],
                                       bk(PS, 7)[:, :SC], ALU.mult)
            ins.then_inc(p.sems["dv"], 1)
    p.block.vector(dv_f)

    p.ln_prev = {"pe": peb + 4 * nsc, "ac": acb + 2 * nsc,
                 "dv": dvb + 2 * nsc, "po": pob + nsc}
    p.bs_owner[1] = [("ac", acb + 2 * nsc), ("dv", dvb + 2 * nsc)]


def emit_rope_proj(p, *, wdma, lhsT_view, KC, rhs_view, dest, rp, pe_wait_pv=None,
                   NTC=4, tok0=0):
    PS = p.PS
    gqb, peb, dvb, acb = p.c["gq"], p.c["pe"], p.c["dv"], p.c["ac"]
    p.c["gq"] += 16
    p.c["pe"] += 1 + NTC
    p.c["dv"] += NTC
    p.c["ac"] += NTC

    p.block.gpsimd(wdma)

    bso = list(p.bs_owner)

    def pe_f(pe):
        p.w(pe, "pe", "gq", gqb + 16)
        if pe_wait_pv is not None:
            pe_wait_pv(pe)
        p.wl(pe, "pe", bso[0])
        p.wl(pe, "pe", bso[1])
        ins = None
        for kc in range(KC):
            lt = lhsT_view(kc)
            for tc in range(NTC):
                ins = pe.matmul(bk(PS, tc)[0:64], lt, rhs_view(kc, tc),
                                start=(kc == 0), stop=(kc == KC - 1))
        ins.then_inc(p.sems["pe"], 1)
        for tc in range(NTC):
            par = tc % 2
            p.w(pe, "pe", "dv", dvb + tc + 1)
            pe.matmul(bk(PS, 4 + tc)[0:64], rp["rmix"][:, 0:64],
                      rp["v_sb"][0:64, par * 512:(par + 1) * 512],
                      start=True, stop=False)
            pe.matmul(bk(PS, 4 + tc)[0:64], rp["rmix"][:, 64:128],
                      rp["w_sb"][0:64, par * 512:(par + 1) * 512],
                      start=False, stop=True).then_inc(p.sems["pe"], 1)
    p.block.tensor(pe_f)

    def dv_f(dv):
        for tc in range(NTC):
            par = tc % 2
            p.w(dv, "dv", "pe", peb + 1)
            if tc >= 2:
                p.w(dv, "dv", "pe", peb + 1 + tc - 1)
            else:
                p.w(dv, "dv", "pe", peb)
            dv.tensor_tensor(rp["v_sb"][0:64, par * 512:(par + 1) * 512],
                             bk(PS, tc)[0:64],
                             rp["csk"][:, tok0 + tc * 512:tok0 + (tc + 1) * 512], ALU.mult)
            dv.tensor_tensor(rp["w_sb"][0:64, par * 512:(par + 1) * 512],
                             bk(PS, tc)[0:64],
                             rp["sck"][:, tok0 + tc * 512:tok0 + (tc + 1) * 512],
                             ALU.mult).then_inc(p.sems["dv"], 1)
    p.block.vector(dv_f)

    def ac_f(ac):
        for tc in range(NTC):
            p.w(ac, "ac", "pe", peb + 1 + tc + 1)
            ac.activation(dest[0:64, tok0 + tc * 512:tok0 + (tc + 1) * 512],
                          bk(PS, 4 + tc)[0:64],
                          AF.Copy).then_inc(p.sems["ac"], 1)
    p.block.scalar(ac_f)

    p.bs_owner = [[("dv", dvb + NTC)], [("ac", acb + NTC)]]


def emit_qr_pairs(p, *, wdma_pair, lhsT_view, rhs_view, W, rp2, qstg, qr_scr,
                  c0, sq_free=None):
    PS = p.PS
    gqb, peb, dvb, acb, sqb = p.c["gq"], p.c["pe"], p.c["dv"], p.c["ac"], p.c["sq"]
    p.c["gq"] += 16 * 8
    p.c["pe"] += 16
    p.c["dv"] += 8
    p.c["ac"] += 8
    p.c["sq"] += 16 * 8
    wslf = list(p.wsl_free)
    bso = list(p.bs_owner)

    def gp_f(gp):
        for j in range(8):
            if j >= 2:
                p.w(gp, "gp", "pe", peb + 2 * (j - 1) - 1)
            else:
                p.wl(gp, "gp", wslf[j % 2])
            wdma_pair(gp, j)
    p.block.gpsimd(gp_f)

    def pe_f(pe):
        for j in range(8):
            p.w(pe, "pe", "gq", gqb + 16 * (j + 1))
            if j >= 4:
                p.w(pe, "pe", "dv", dvb + j - 3)
            else:
                p.wl(pe, "pe", bso[0])
            ins = None
            for kc in range(KCE):
                ins = pe.matmul(bk(PS, j % 4)[:, 0:W], lhsT_view(j, kc),
                                rhs_view(kc), start=(kc == 0), stop=(kc == KCE - 1))
            ins.then_inc(p.sems["pe"], 1)
            p.w(pe, "pe", "dv", dvb + j + 1)
            if j >= 4:
                p.w(pe, "pe", "ac", acb + j - 3)
            else:
                p.wl(pe, "pe", bso[1])
            pe.matmul(bk(PS, 4 + j % 4)[:, 0:W], rp2["rmix"][:, 0:128],
                      rp2["v_sb"][:, (j % 2) * 512:(j % 2) * 512 + W],
                      start=True, stop=False)
            pe.matmul(bk(PS, 4 + j % 4)[:, 0:W], rp2["rmix"][:, 128:256],
                      rp2["w_sb"][:, (j % 2) * 512:(j % 2) * 512 + W],
                      start=False, stop=True).then_inc(p.sems["pe"], 1)
    p.block.tensor(pe_f)

    def dv_f(dv):
        for j in range(8):
            p.w(dv, "dv", "pe", peb + 2 * j + 1)
            if j < 2:
                p.w(dv, "dv", "pe", peb)
            dv.tensor_tensor(rp2["v_sb"][:, (j % 2) * 512:(j % 2) * 512 + W],
                             bk(PS, j % 4)[:, 0:W],
                             rp2["csk"][:, c0:c0 + W], ALU.mult)
            dv.tensor_tensor(rp2["w_sb"][:, (j % 2) * 512:(j % 2) * 512 + W],
                             bk(PS, j % 4)[:, 0:W],
                             rp2["sck"][:, c0:c0 + W],
                             ALU.mult).then_inc(p.sems["dv"], 1)
    p.block.vector(dv_f)

    def ac_f(ac):
        for j in range(8):
            p.w(ac, "ac", "pe", peb + 2 * j + 2)
            if j >= 2:
                p.w(ac, "ac", "sq", sqb + 16 * (j - 1))
            elif sq_free is not None:
                p.wl(ac, "ac", sq_free[j % 2])
            ac.activation(qstg[:, (j % 2) * 512:(j % 2) * 512 + W],
                          bk(PS, 4 + j % 4)[:, 0:W],
                          AF.Copy).then_inc(p.sems["ac"], 1)
    p.block.scalar(ac_f)

    def sy_f(sy):
        for j in range(8):
            p.w(sy, "sy", "ac", acb + j + 1)
            sy.dma_start(out=qr_scr[j * 128:(j + 1) * 128, c0:c0 + W],
                         in_=qstg[:, (j % 2) * 512:(j % 2) * 512 + W]
                         ).then_inc(p.sems["sq"], 16)
    p.block.sync(sy_f)

    p.wsl_free = [[("pe", peb + 16)], [("pe", peb + 16)]]
    p.bs_owner = [[("dv", dvb + 8)], [("ac", acb + 8)]]
    return [("sq", sqb + 16 * 7)], [("sq", sqb + 16 * 8)]


def emit_attn_a_head(p, h, hb, ten):
    PS = p.PS
    gqb, sqb = p.c["gq"], p.c["sq"]
    peb, acb, dvb = p.c["pe"], p.c["ac"], p.c["dv"]
    ntks = [4 * qb + 4 for qb in range(4)]
    PEB, ACB, DVB = 5, 5, 0
    pe_qb0 = [peb + PEB + sum(nt + 3 for nt in ntks[:qb]) for qb in range(5)]
    ac_qb0 = [acb + ACB + sum(nt + 2 for nt in ntks[:qb]) for qb in range(5)]
    dv_qb0 = [dvb + DVB + 5 * qb for qb in range(5)]
    p.c["gq"] += 16 * 2
    p.c["sq"] += 16 * 2 + 16 * 4
    p.c["pe"] = pe_qb0[4]
    p.c["ac"] = ac_qb0[4]
    p.c["dv"] = dv_qb0[4]
    sqq = sqb + 32
    ao_sq0 = sqb + 32

    scores_inc = lambda qb, tk: pe_qb0[qb] + tk + 1
    sums_stop = lambda qb: pe_qb0[qb] + ntks[qb] + 1
    attnv_stop = lambda qb: pe_qb0[qb] + ntks[qb] + 2
    rsbc_inc = lambda qb: pe_qb0[qb] + ntks[qb] + 3
    exp_inc = lambda qb, tk: ac_qb0[qb] + tk + 1
    sumsln_inc = lambda qb: ac_qb0[qb] + ntks[qb] + 1
    rscopy_inc = lambda qb: ac_qb0[qb] + ntks[qb] + 2
    mask_inc = lambda qb, d: dv_qb0[qb] + d + 1
    aoev_inc = lambda qb: dv_qb0[qb] + 5
    par2 = lambda qb: qb % 2

    sl = h % 2
    qch, qrh = ten["qch2"][sl], ten["qrh2"][sl]
    kh, vtm, probs = ten["kh"], ten["vtm"], ten["probs"]
    ckv, kR = ten["ckv"], ten["kR"]
    bso = [list(p.bs_owner[0] or []), list(p.bs_owner[1] or [])]

    def sy_q(sy):
        p.w(sy, "sy", "sq", sqb)
        if hb.get("hfin") and len(hb["hfin"]) >= 2:
            p.w(sy, "sy", "pe", hb["hfin"][-2])
        elif hb.get("qfree") is not None:
            p.w(sy, "sy", "pe", hb["qfree"])
        sy.dma_start(out=qch[:, :],
                     in_=ten["qc_scr"][h * 128:(h + 1) * 128, :]).then_inc(p.sems["sq"], 16)
        sy.dma_start(out=qrh[0:64, :],
                     in_=ten["qr_scr"][h * 64:(h + 1) * 64, :]).then_inc(p.sems["sq"], 16)
    p.block.sync(sy_q)

    def gp_f(gp):
        if hb.get("wfree") is not None:
            p.w(gp, "gp", "pe", hb["wfree"])
        gp.dma_start(out=ten["wuk_sb"][sl].ap().rearrange("q (k m) -> q k m", m=128),
                     in_=ten["wuk"][h]).then_inc(p.sems["gq"], 16)
        gp.dma_start(out=ten["wuv_sb"][sl].ap().rearrange("q (k m) -> q k m", m=128),
                     in_=ten["wuvr"][h].rearrange("k q m -> q k m")).then_inc(p.sems["gq"], 16)
    p.block.gpsimd(gp_f)
    hb["wfree"] = peb + PEB
    hb.setdefault("hfin", []).append(pe_qb0[4])

    def pe_f(pe):
        p.w(pe, "pe", "gq", gqb + 32)
        if h > 0:
            p.w(pe, "pe", "ac", acb)
            p.w(pe, "pe", "dv", dvb)
        else:
            p.wl(pe, "pe", bso[0])
            p.wl(pe, "pe", bso[1])
        ins = None
        for kc in range(4):
            for tc in range(4):
                ins = pe.matmul(bk(PS, tc), ten["wuk_sb"][sl][:, kc * 128:(kc + 1) * 128],
                                ckv[:, kc * S + tc * 512: kc * S + (tc + 1) * 512],
                                start=(kc == 0), stop=(kc == 3))
        ins.then_inc(p.sems["pe"], 1)
        for mg in range(4):
            for mm in range(4):
                m = mg * 4 + mm
                for kc in range(4):
                    ins = pe.matmul(bk(PS, 4 + mg)[:, mm * 128:(mm + 1) * 128],
                                    ckv[:, kc * S + m * 128: kc * S + (m + 1) * 128],
                                    ten["wuv_sb"][sl][:, kc * 128:(kc + 1) * 128],
                                    start=(kc == 0), stop=(kc == 3))
            ins.then_inc(p.sems["pe"], 1)
        p.w(pe, "pe", "sq", sqq)
        for qb in range(4):
            ntk = ntks[qb]
            qsl = slice(qb * 512, (qb + 1) * 512)

            def scores(tk):
                if tk >= 2:
                    p.w(pe, "pe", "ac", exp_inc(qb, tk - 2))
                else:
                    p.w(pe, "pe", "ac", ac_qb0[qb])
                pe.matmul(bk(PS, tk % 2), kh[:, tk * 128:(tk + 1) * 128],
                          qch[:, qsl], start=True, stop=False)
                pe.matmul(bk(PS, tk % 2), kR[0:64, tk * 128:(tk + 1) * 128],
                          qrh[0:64, qsl], start=False,
                          stop=True).then_inc(p.sems["pe"], 1)

            def consume(tk):
                if tk >= 4 * qb:
                    p.w(pe, "pe", "dv", mask_inc(qb, tk - 4 * qb))
                else:
                    p.w(pe, "pe", "ac", exp_inc(qb, tk))
                if tk == 0 and qb >= 2:
                    p.w(pe, "pe", "ac", sumsln_inc(qb - 2))
                    p.w(pe, "pe", "dv", aoev_inc(qb - 2))
                pr = probs[:, tk * 512:(tk + 1) * 512]
                ins0 = pe.matmul(bk(PS, 2 + par2(qb))[0:1], ten["ones_b"], pr,
                                 start=(tk == 0), stop=(tk == ntk - 1))
                if tk == ntk - 1:
                    ins0.then_inc(p.sems["pe"], 1)
                ins1 = pe.matmul(bk(PS, 4 + par2(qb)), vtm[:, tk * 128:(tk + 1) * 128],
                                 pr, start=(tk == 0), stop=(tk == ntk - 1))
                if tk == ntk - 1:
                    ins1.then_inc(p.sems["pe"], 1)

            scores(0)
            if ntk > 1:
                scores(1)
            for tk in range(ntk):
                consume(tk)
                if tk + 2 < ntk:
                    scores(tk + 2)
            p.w(pe, "pe", "ac", sumsln_inc(qb))
            if qb >= 2:
                p.w(pe, "pe", "ac", rscopy_inc(qb - 2))
            pe.matmul(bk(PS, 6 + par2(qb)), ten["ones_fr"], ten["rs_row"],
                      start=True, stop=True).then_inc(p.sems["pe"], 1)
    p.block.tensor(pe_f)

    def ac_f(ac):
        p.w(ac, "ac", "pe", peb + 1)
        ac.activation(kh[:, :], PS[:, 0:2048], AF.Copy).then_inc(p.sems["ac"], 1)
        for mg in range(4):
            p.w(ac, "ac", "pe", peb + 2 + mg)
            ac.activation(vtm[:, mg * 512:(mg + 1) * 512], bk(PS, 4 + mg),
                          AF.Copy).then_inc(p.sems["ac"], 1)
        for qb in range(4):
            ntk = ntks[qb]
            for tk in range(ntk):
                p.w(ac, "ac", "pe", scores_inc(qb, tk))
                ac.activation(probs[:, tk * 512:(tk + 1) * 512], bk(PS, tk % 2),
                              AF.Exp, bias=ten["c_negB"],
                              scale=SCALE_A).then_inc(p.sems["ac"], 1)
            p.w(ac, "ac", "pe", attnv_stop(qb))
            if qb >= 1:
                p.w(ac, "ac", "pe", rsbc_inc(qb - 1))
            ac.activation(ten["rs_row"], bk(PS, 2 + par2(qb))[0:1], AF.Ln,
                          bias=ten["c_zero"])
            ac.activation(ten["rs_row"], ten["rs_row"], AF.Exp,
                          scale=-1.0).then_inc(p.sems["ac"], 1)
            p.w(ac, "ac", "pe", rsbc_inc(qb))
            if qb >= 2:
                p.w(ac, "ac", "dv", aoev_inc(qb - 2))
            ac.activation(ten["rs_sb"][:, par2(qb) * 512:(par2(qb) + 1) * 512],
                          bk(PS, 6 + par2(qb)), AF.Copy).then_inc(p.sems["ac"], 1)
    p.block.scalar(ac_f)

    def dv_f(dv):
        for qb in range(4):
            for d in range(4):
                tk = 4 * qb + d
                p.w(dv, "dv", "ac", exp_inc(qb, tk))
                dv.tensor_tensor(probs[:, tk * 512:(tk + 1) * 512],
                                 probs[:, tk * 512:(tk + 1) * 512],
                                 ten["maska_sb"][:, d * 512:(d + 1) * 512],
                                 ALU.mult).then_inc(p.sems["dv"], 1)
            p.w(dv, "dv", "pe", attnv_stop(qb))
            p.w(dv, "dv", "ac", rscopy_inc(qb))
            if qb >= 2:
                p.w(dv, "dv", "sq", ao_sq0 + 16 * (qb - 1))
            elif hb.get("aofree") is not None:
                p.w(dv, "dv", "sq", hb["aofree"][par2(qb)])
            dv.tensor_tensor(ten["aosl"][:, par2(qb) * 512:(par2(qb) + 1) * 512],
                             bk(PS, 4 + par2(qb)),
                             ten["rs_sb"][:, par2(qb) * 512:(par2(qb) + 1) * 512],
                             ALU.mult).then_inc(p.sems["dv"], 1)
    p.block.vector(dv_f)

    def sy_f(sy):
        for qb in range(4):
            p.w(sy, "sy", "dv", aoev_inc(qb))
            sy.dma_start(out=ten["ao_scr"][h * 128:(h + 1) * 128, qb * 512:(qb + 1) * 512],
                         in_=ten["aosl"][:, par2(qb) * 512:(par2(qb) + 1) * 512]
                         ).then_inc(p.sems["sq"], 16)
    p.block.sync(sy_f)
    hb["aofree"] = [ao_sq0 + 16 * 3, ao_sq0 + 16 * 4]


def emit_attn_sw(p, hl, hb, ten):
    PS = p.PS
    S_ = S
    peb, acb, dvb = p.c["pe"], p.c["ac"], p.c["dv"]
    ntks = [len(tl) for tl in TLW]
    pe_qb0 = [peb + sum(ntks[:qb]) + 3 * qb for qb in range(5)]
    ac_qb0 = [acb + sum(ntks[:qb]) + 2 * qb for qb in range(5)]
    dv_qb0 = [dvb + sum(ntks[:qb]) + 1 * qb for qb in range(5)]
    p.c["pe"] = pe_qb0[4]
    p.c["ac"] = ac_qb0[4]
    p.c["dv"] = dv_qb0[4]

    scores_inc = lambda qb, j: pe_qb0[qb] + j + 1
    sums_stop = lambda qb: pe_qb0[qb] + ntks[qb] + 1
    attnv_stop = lambda qb: pe_qb0[qb] + ntks[qb] + 2
    rsbc_inc = lambda qb: pe_qb0[qb] + ntks[qb] + 3
    exp_inc = lambda qb, j: ac_qb0[qb] + j + 1
    sumsln_inc = lambda qb: ac_qb0[qb] + ntks[qb] + 1
    rscopy_inc = lambda qb: ac_qb0[qb] + ntks[qb] + 2
    mask_inc = lambda qb, j: dv_qb0[qb] + j + 1
    aoev_inc = lambda qb: dv_qb0[qb] + ntks[qb] + 1
    par = lambda qb: qb % 2

    Qg, Kg, vtm, probs = ten["Qg"], ten["Kg"], ten["vtm_g"], ten["probs"]
    ao_g = ten["ao_g"]
    bso = [list(p.bs_owner[0] or []), list(p.bs_owner[1] or [])]

    def pe_f(pe):
        for qb in range(4):
            tl = TLW[qb]
            ntk = ntks[qb]
            qsl = slice(hl * S_ + qb * 512, hl * S_ + (qb + 1) * 512)

            def scores(j):
                if j >= 2:
                    p.w(pe, "pe", "ac", exp_inc(qb, j - 2))
                elif qb == 0 and hb.get("first"):
                    p.wl(pe, "pe", bso[0])
                    p.wl(pe, "pe", bso[1])
                else:
                    p.w(pe, "pe", "ac", ac_qb0[qb])
                t = tl[j]
                pe.matmul(bk(PS, j % 2), Kg[:, hl * S_ + t * 128:hl * S_ + (t + 1) * 128],
                          Qg[:, qsl], start=True, stop=True).then_inc(p.sems["pe"], 1)

            def consume(j):
                p.w(pe, "pe", "dv", mask_inc(qb, j))
                if j == 0:
                    if qb >= 2:
                        p.w(pe, "pe", "ac", sumsln_inc(qb - 2))
                        p.w(pe, "pe", "dv", aoev_inc(qb - 2))
                    else:
                        p.w(pe, "pe", "ac", hb["sumsln"][qb])
                        p.w(pe, "pe", "dv", hb["aoev"][qb])
                t = tl[j]
                sl_ = j + 16 * (qb % 2)
                pr = probs[:, sl_ * 512:(sl_ + 1) * 512]
                ins0 = pe.matmul(bk(PS, 2 + par(qb))[0:1], ten["ones_b"], pr,
                                 start=(j == 0), stop=(j == ntk - 1))
                if j == ntk - 1:
                    ins0.then_inc(p.sems["pe"], 1)
                ins1 = pe.matmul(bk(PS, 4 + par(qb)),
                                 vtm[:, (t * 8 + hl) * 128:(t * 8 + hl + 1) * 128],
                                 pr, start=(j == 0), stop=(j == ntk - 1))
                if j == ntk - 1:
                    ins1.then_inc(p.sems["pe"], 1)

            scores(0)
            scores(1)
            for j in range(ntk):
                consume(j)
                if j + 2 < ntk:
                    scores(j + 2)
            p.w(pe, "pe", "ac", sumsln_inc(qb))
            if qb >= 2:
                p.w(pe, "pe", "ac", rscopy_inc(qb - 2))
            else:
                p.w(pe, "pe", "ac", hb["rscopy"][qb])
            pe.matmul(bk(PS, 6 + par(qb)), ten["ones_fr"], ten["rs_row"],
                      start=True, stop=True).then_inc(p.sems["pe"], 1)
    p.block.tensor(pe_f)

    def ac_f(ac):
        for qb in range(4):
            ntk = ntks[qb]
            for j in range(ntk):
                p.w(ac, "ac", "pe", scores_inc(qb, j))
                if j == 0:
                    p.w(ac, "ac", "pe",
                        attnv_stop(qb - 2) if qb >= 2 else hb["attnv3"])
                sl_ = j + 16 * (qb % 2)
                ac.activation(probs[:, sl_ * 512:(sl_ + 1) * 512], bk(PS, j % 2),
                              AF.Exp, bias=ten["c_negB"],
                              scale=SCALE_B).then_inc(p.sems["ac"], 1)
            p.w(ac, "ac", "pe", attnv_stop(qb))
            if qb >= 1:
                p.w(ac, "ac", "pe", rsbc_inc(qb - 1))
            ac.activation(ten["rs_row"], bk(PS, 2 + par(qb))[0:1], AF.Ln,
                          bias=ten["c_zero"])
            ac.activation(ten["rs_row"], ten["rs_row"], AF.Exp,
                          scale=-1.0).then_inc(p.sems["ac"], 1)
            p.w(ac, "ac", "pe", rsbc_inc(qb))
            if qb >= 2:
                p.w(ac, "ac", "dv", aoev_inc(qb - 2))
            else:
                p.w(ac, "ac", "dv", hb["aoev"][qb])
            ac.activation(ten["rs_sb"][:, par(qb) * 512:(par(qb) + 1) * 512],
                          bk(PS, 6 + par(qb)), AF.Copy).then_inc(p.sems["ac"], 1)
    p.block.scalar(ac_f)

    def dv_f(dv):
        for qb in range(4):
            tl = TLW[qb]
            ntk = ntks[qb]
            for j in range(ntk):
                p.w(dv, "dv", "ac", exp_inc(qb, j))
                mi = maskw_idx(qb, tl[j])
                sl_ = j + 16 * (qb % 2)
                dv.tensor_tensor(probs[:, sl_ * 512:(sl_ + 1) * 512],
                                 probs[:, sl_ * 512:(sl_ + 1) * 512],
                                 ten["maskw_sb"][:, mi * 512:(mi + 1) * 512],
                                 ALU.mult).then_inc(p.sems["dv"], 1)
            p.w(dv, "dv", "pe", attnv_stop(qb))
            p.w(dv, "dv", "ac", rscopy_inc(qb))
            dv.tensor_tensor(ao_g[:, hl * S_ + qb * 512:hl * S_ + (qb + 1) * 512],
                             bk(PS, 4 + par(qb)),
                             ten["rs_sb"][:, par(qb) * 512:(par(qb) + 1) * 512],
                             ALU.mult).then_inc(p.sems["dv"], 1)
    p.block.vector(dv_f)

    hb["first"] = False
    hb["attnv3"] = attnv_stop(3)
    hb["sumsln"] = [sumsln_inc(2), sumsln_inc(3)]
    hb["aoev"] = [aoev_inc(2), aoev_inc(3)]
    hb["rscopy"] = [rscopy_inc(2), rscopy_inc(3)]



def build_nc():
    nc = bass.Bass("TRN2", target_bir_lowering=False)
    ap_ = lambda name, shape, dt=F32: nc.dram_tensor(name, list(shape), dt,
                                                     kind="ExternalInput").ap()
    x_in = ap_("x", [NSEQ, E, S])
    wdq = ap_("wdq", [16, 128, 16, 128]); wdkv = ap_("wdkv", [4, 128, 16, 128])
    wkr = ap_("wkr", [1, 128, 16, 64]); wuq = ap_("wuq", [16, 128, 16, 128])
    wqr2 = ap_("wqr2", [8, 128, 16, 128]); wuk = ap_("wuk", [16, 128, 4, 128])
    wuvr = ap_("wuvr", [16, 4, 128, 128]); wo = ap_("wo", [16, 128, 16, 128])
    wq = ap_("wq", [16, 128, 16, 128]); wk = ap_("wk", [16, 128, 16, 128])
    wvr = ap_("wvr", [2, 16, 128, 1024]); woo = ap_("woo", [16, 128, 16, 128])
    w1 = ap_("w1", [32, 128, 16, 128]); w2 = ap_("w2", [32, 128, 32, 128])
    w3 = ap_("w3", [16, 128, 32, 128])
    b1c = ap_("b1c", [128, 32]); b2c = ap_("b2c", [128, 32]); b3r = ap_("b3r", [1, E])
    csk_h = ap_("csk2", [128, S]); sck_h = ap_("sck2", [128, S])
    rmix_h = ap_("rmix", [64, 128], BF16)
    rmix2_h = ap_("rmix2", [128, 256], BF16)
    maska_h = ap_("maska", [4, 128, 512], BF16)
    maskw_h = ap_("maskw", [16, 128, 512], BF16)

    y_out = nc.dram_tensor("y", [NSEQ, E, S], F32, kind="ExternalOutput").ap()
    X_int = nc.dram_tensor("Xint", [NSEQ, E, S], F32, kind="Internal").ap()
    ao_scr = nc.dram_tensor("aoscr", [E, S], BF16, kind="Internal").ap()
    qc_scr = nc.dram_tensor("qcscr", [E, S], BF16, kind="Internal").ap()
    qr_scr = nc.dram_tensor("qrscr", [1024, S], BF16, kind="Internal").ap()

    top = contextlib.ExitStack()
    if True:
        En = top.enter_context
        PS = En(nc.psum_tensor("PS", [128, 4096], F32))
        ones_b = En(nc.sbuf_tensor("ones_b", [128, 1], BF16))
        ones_fr = En(nc.sbuf_tensor("ones_fr", [1, 128], F32))
        ones_br = En(nc.sbuf_tensor("ones_br", [1, 512], BF16))
        c_negB = En(nc.sbuf_tensor("c_negB", [128, 1], F32))
        c_eps = En(nc.sbuf_tensor("c_eps", [1, 1], F32))
        c_zero = En(nc.sbuf_tensor("c_zero", [1, 1], F32))
        b1_sb = En(nc.sbuf_tensor("b1_sb", [128, 32], F32))
        b2_sb = En(nc.sbuf_tensor("b2_sb", [128, 32], F32))
        b3_sb = En(nc.sbuf_tensor("b3_sb", [1, E], BF16))
        maska_sb = En(nc.sbuf_tensor("maska_sb", [128, 4 * 512], BF16))
        maskw_sb = En(nc.sbuf_tensor("maskw_sb", [128, 16 * 512], BF16))
        rs_sb = En(nc.sbuf_tensor("rs_sb", [128, 2 * 512], F32))
        rows_sb = En(nc.sbuf_tensor("rows_sb", [1, 1536], F32))
        ts_sb = En(nc.sbuf_tensor("ts_sb", [128, 256], F32))
        rmix_sb = En(nc.sbuf_tensor("rmix_sb", [64, 128], BF16))
        rmix2_sb = En(nc.sbuf_tensor("rmix2_sb", [128, 256], BF16))
        wslA = En(nc.sbuf_tensor("wslA", [128, 4096], BF16))
        wslB = En(nc.sbuf_tensor("wslB", [128, 4096], BF16))

        sems = {s: En(nc.semaphore(s)) for s in ["gq", "sq", "pe", "dv", "ac", "po"]}
        block = En(nc.Block())
        p = P(nc, block, sems)
        p.PS = PS[:, :]
        wsl = [wslA, wslB]
        _uid = [0]

        import functools

        def sb(nm, shape, dtype, **kw):
            _uid[0] += 1
            cm = nc.sbuf_tensor(f"{nm}_{_uid[0]}", shape, dtype, **kw)

            class _Wrap:
                def __enter__(self):
                    h = cm.__enter__()
                    try:
                        mloc = nc.lookup_mloc(h)
                        per = 1
                        for d in h.shape[1:]:
                            per *= d
                        per *= mybir.dt.size(h.dtype)
                        _ALLOCLOG.append(("A", h.name, mloc.addr, per))
                    except Exception as e:
                        _ALLOCLOG.append(("A?", f"{nm}_{_uid[0]}", str(e), 0))
                    self._h = h
                    return h

                def __exit__(self, *a):
                    _ALLOCLOG.append(("F", self._h.name, 0, 0))
                    return cm.__exit__(*a)
            return _Wrap()

        ln_base = {
            "ones_b": ones_b[:, :], "ones_fr": ones_fr[:, :],
            "m_row": rows_sb[0:1, 0:256], "s2_row": rows_sb[0:1, 256:512],
            "var_row": rows_sb[0:1, 512:768], "r_row": rows_sb[0:1, 768:1024],
            "ts": ts_sb[:, :], "c_eps": c_eps[0:1, 0:1],
        }
        rs_row = rows_sb[0:1, 1024:1536]

        p.c["sq"] += 16 * 6

        def init_sy(sy):
            sy.dma_start(out=b1_sb[:, :], in_=b1c).then_inc(sems["sq"], 16)
            sy.dma_start(out=b2_sb[:, :], in_=b2c).then_inc(sems["sq"], 16)
            sy.dma_start(out=maska_sb.ap().rearrange("q (d n) -> q d n", n=512),
                         in_=maska_h.rearrange("d q n -> q d n")).then_inc(sems["sq"], 16)
            sy.dma_start(out=maskw_sb.ap().rearrange("q (d n) -> q d n", n=512),
                         in_=maskw_h.rearrange("d q n -> q d n")).then_inc(sems["sq"], 16)
            sy.dma_start(out=rmix_sb[:, :], in_=rmix_h).then_inc(sems["sq"], 16)
            sy.dma_start(out=rmix2_sb[:, :], in_=rmix2_h).then_inc(sems["sq"], 16)
        block.sync(init_sy)

        p.c["gq"] += 16 * (1 + NSEQ)

        def init_gp(gp):
            gp.dma_start(out=b3_sb[:, :], in_=b3r).then_inc(sems["gq"], 16)
            for sq in range(NSEQ):
                gp.dma_start(out=X_int[sq], in_=x_in[sq]).then_inc(sems["gq"], 16)
        block.gpsimd(init_gp)

        p.c["dv"] += 1

        def init_dv(dv):
            dv.memset(ones_b[:, :], 1.0)
            dv.memset(ones_fr[:, :], 1.0)
            dv.memset(ones_br[:, :], 1.0)
            dv.memset(c_negB[:, :], NEG_EXP_BIAS)
            dv.memset(c_zero[:, :], 0.0)
            dv.memset(c_eps[:, :], EPS).then_inc(sems["dv"], 1)
        block.vector(init_dv)
        p.barrier()
        if _STOP_AFTER == "init":
            snap0 = dict(p.c)

            def fin0(sy):
                for sname, v in snap0.items():
                    p.w(sy, "sy", sname, v)
            block.sync(fin0)
            block.__exit__(None, None, None)
            nc._top_stack = top
            return nc, p

        def alloc_ln(stk, nq=2, sc=256):
            t = dict(ln_base)
            t["xf32"] = [stk.enter_context(sb(f"ln_xf{i}", [128, KCE * sc], F32, side="right")) for i in range(2)]
            t["xsq"] = [stk.enter_context(sb(f"ln_xq{i}", [128, KCE * sc], BF16, side="right")) for i in range(nq)]
            return t

        def std_wsl(ph, n=16):
            return [ph.enter_context(sb(f"wsl{i}", [128, n * 128], BF16))
                    for i in range(2)]

        def ffn_app(sq, dest_hbm, src_ready=None):
            Xs = X_int[sq]
            for ch in range(2):
                c0 = ch * 1024
                with contextlib.ExitStack() as stk:
                    with contextlib.ExitStack() as xstk:
                        xbc = xstk.enter_context(sb("xbc", [128, KCE * 1024], BF16,
                                                                side="right"))
                        with contextlib.ExitStack() as lns:
                            t = alloc_ln(lns)
                            emit_ln(p, t, Xs,
                                    lambda kc, cc, w: xbc[:, kc * 1024 + cc: kc * 1024 + cc + w],
                                    lambda cc, w: xbc.ap().rearrange("q (kc n) -> q kc n", n=1024)[:, :, cc:cc + w],
                                    c0, 1024, src_ready=src_ready,
                                    buf_ready=[("pe", p.c["pe"]),
                                               ("sq", p.c["sq"])])
                            ln_dv = p.ln_prev["dv"]
                            p.mb()
                            if ch == 1:
                                p.tap("fxb", xbc[:, :], [128, KCE * 1024], BF16)
                                p.tap("fxf0", t["xf32"][0][:, :], [128, KCE * 256], F32)
                                p.tap("fxq0", t["xsq"][0][:, :], [128, KCE * 256], BF16)
                        h1 = stk.enter_context(sb("h1", [128, 32 * 1024], BF16))

                        def dma_w1(gp, bi):
                            gp.dma_start(out=wsl[bi % 2].ap().rearrange(
                                "q (k m) -> q k m", m=128)[:, 0:16, :],
                                in_=w1[bi]).then_inc(sems["gq"], 16)
                        emit_proj(
                            p, n_oc=32, wdma=dma_w1, nw=32, wbatch=1,
                            lhsT_view=lambda oc, kc: wsl[oc % 2][:, kc * 128:(kc + 1) * 128],
                            KC=KCE,
                            rhs_view=lambda kc, tc: xbc[:, kc * 1024 + tc * 512: kc * 1024 + (tc + 1) * 512],
                            NTC=2,
                            evac=lambda eng, oc, st_, ps: eng.activation(
                                h1[:, oc * 1024:(oc + 1) * 1024], ps, AF.Relu,
                                bias=b1_sb[:, oc:oc + 1]),
                            evac_eng="ac",
                            pe_wait0=lambda pe: p.w(pe, "pe", "dv", ln_dv))
                        h1_ac = p.c["ac"]
                        p.mb()
                        p.tap(f"fh1c{ch}", h1[:, :], [128, 32 * 1024], BF16)
                    h2 = stk.enter_context(sb("h2", [128, 32 * 1024], BF16))

                    def dma_w2(gp, bi):
                        gp.dma_start(out=wsl[bi % 2].ap().rearrange(
                            "q (k m) -> q k m", m=128), in_=w2[bi]).then_inc(sems["gq"], 16)
                    emit_proj(
                        p, n_oc=32, wdma=dma_w2, nw=32, wbatch=1,
                        lhsT_view=lambda oc, kc: wsl[oc % 2][:, kc * 128:(kc + 1) * 128],
                        KC=32,
                        rhs_view=lambda kc, tc: h1[:, kc * 1024 + tc * 512: kc * 1024 + (tc + 1) * 512],
                        NTC=2,
                        evac=lambda eng, oc, st_, ps: eng.activation(
                            h2[:, oc * 1024:(oc + 1) * 1024], ps, AF.Relu,
                            bias=b2_sb[:, oc:oc + 1]),
                        evac_eng="ac",
                        pe_wait0=(lambda v: lambda pe: p.w(pe, "pe", "ac", v))(h1_ac))
                    h2_ac = p.c["ac"]
                    p.mb()
                    p.tap(f"fh2c{ch}", h2[:, :], [128, 32 * 1024], BF16)

                    of32f = stk.enter_context(sb("of32f", [128, 2 * 1024], F32))

                    def dma_w3(gp, bi):
                        gp.dma_start(out=wsl[bi % 2].ap().rearrange(
                            "q (k m) -> q k m", m=128), in_=w3[bi]).then_inc(sems["gq"], 16)

                    def rank1_b3(pe, oc, tc, psb):
                        pe.matmul(psb, b3_sb[0:1, oc * 128:(oc + 1) * 128],
                                  ones_br[0:1, 0:512], start=True, stop=False)
                    emit_proj_accum(
                        p, n_oc=16, wdma=dma_w3, nw=16, wbatch=1,
                        lhsT_view=lambda oc, kc: wsl[oc % 2][:, kc * 128:(kc + 1) * 128],
                        KC=32,
                        rhs_view=lambda kc, tc: h2[:, kc * 1024 + tc * 512: kc * 1024 + (tc + 1) * 512],
                        NTC=2,
                        dest_piece=lambda oc, pc: dest_hbm[oc * 128:(oc + 1) * 128, c0:c0 + 1024],
                        of32=of32f[:, :], rank1=rank1_b3,
                        pe_wait0=(lambda v: lambda pe: p.w(pe, "pe", "ac", v))(h2_ac))
                    p.mb()
                src_ready = [("gq", p.c["gq"])]

        try:
          for sq in range(NSEQ):
            Xs = X_int[sq]

            sl1_src_ready = [("gq", p.c["gq"])]
            with contextlib.ExitStack() as sl:
                csk2_sb = sl.enter_context(sb("csk2s", [128, S], F32))
                sck2_sb = sl.enter_context(sb("sck2s", [128, S], F32))
                v2_sb = sl.enter_context(sb("v2s", [128, 2 * 512], BF16))
                w2_sb = sl.enter_context(sb("w2s", [128, 2 * 512], BF16))
                qstg = sl.enter_context(sb("qstg", [128, 2 * 512], BF16))
                qstg2 = sl.enter_context(sb("qstg2", [128, 2 * 512], BF16))
                rp2 = {"csk": csk2_sb[:, :], "sck": sck2_sb[:, :],
                       "rmix": rmix2_sb[:, :], "v_sb": v2_sb[:, :], "w_sb": w2_sb[:, :]}
                rp = {"csk": csk2_sb[0:64, :], "sck": sck2_sb[0:64, :],
                      "rmix": rmix_sb[:, :], "v_sb": v2_sb[0:64, 0:1024],
                      "w_sb": w2_sb[0:64, 0:1024]}
                p.c["sq"] += 32
                pe_now = p.c["pe"]

                def rope_ld(sy):
                    p.w(sy, "sy", "pe", pe_now)
                    sy.dma_start(out=csk2_sb[:, :], in_=csk_h).then_inc(sems["sq"], 16)
                    sy.dma_start(out=sck2_sb[:, :], in_=sck_h).then_inc(sems["sq"], 16)
                block.sync(rope_ld)
                rope_sq = p.c["sq"]

                def rope_gate(dv):
                    p.w(dv, "dv", "sq", rope_sq)
                block.vector(rope_gate)

                qstk = contextlib.ExitStack()
                ckv = qstk.enter_context(sb("ckv", [128, 4 * S], BF16, side="right"))
                kR = qstk.enter_context(sb("kR", [64, S], BF16, side="right"))
                qr_free = None
                qc_free = [None, None]
                for ch in range(2):
                  c0 = ch * 1024
                  with contextlib.ExitStack() as ph:
                    xb = ph.enter_context(sb("xb", [128, KCE * 1024], BF16))
                    cqc = ph.enter_context(sb("cqc", [128, KCE * 512], BF16))
                    with contextlib.ExitStack() as lns:
                        t = alloc_ln(lns)
                        emit_ln(p, t, Xs,
                                lambda kc, cc, w: xb[:, kc * 1024 + cc: kc * 1024 + cc + w],
                                lambda cc, w: xb.ap().rearrange("q (kc n) -> q kc n", n=1024)[:, :, cc:cc + w],
                                c0, 1024, src_ready=sl1_src_ready,
                                buf_ready=[("pe", p.c["pe"])])
                        sl1_src_ready = None
                        ln_dv1 = p.ln_prev["dv"]
                        ln_dv0 = ln_dv1 - 4
                        p.mb()
                    for half in range(2):
                        hc0 = c0 + half * 512
                        def dma_dq(gp, bi):
                            gp.dma_start(out=wsl[bi % 2].ap()[:, 0:2048].rearrange(
                                "q (k m) -> q k m", m=128), in_=wdq[bi]).then_inc(sems["gq"], 16)
                        emit_proj(
                            p, n_oc=16, wdma=dma_dq, nw=16, wbatch=1,
                            lhsT_view=lambda oc, kc: wsl[oc % 2][:, kc * 128:(kc + 1) * 128],
                            KC=KCE,
                            rhs_view=(lambda half: lambda kc, tc: xb[:, kc * 1024 + half * 512: kc * 1024 + half * 512 + 512])(half),
                            NTC=1,
                            evac=lambda eng, oc, st_, ps: eng.activation(
                                cqc[:, oc * 512:(oc + 1) * 512], ps, AF.Copy),
                            evac_eng="ac",
                            pe_wait0=(lambda v: lambda pe: p.w(pe, "pe", "dv", v))(
                                ln_dv0 if half == 0 else ln_dv1))
                        p.mb()
                        rhs_cqc = lambda kc, tc=0: cqc[:, kc * 512:(kc + 1) * 512]
                        acb_qc = p.c["ac"]
                        sqb_qc = p.c["sq"]
                        qc_prev = list(qc_free)

                        def dma_uq(gp, bi):
                            gp.dma_start(out=wsl[bi % 2].ap()[:, 0:2048].rearrange(
                                "q (k m) -> q k m", m=128), in_=wuq[bi]).then_inc(sems["gq"], 16)

                        def evac_qc(eng, oc, st_, ps):
                            if oc >= 2:
                                p.w(eng, "ac", "sq", sqb_qc + 16 * (oc - 1))
                            else:
                                p.wl(eng, "ac", qc_prev[oc % 2])
                            return eng.activation(
                                qstg2[:, (oc % 2) * 512:(oc % 2) * 512 + 512],
                                ps, AF.Copy)
                        emit_proj(
                            p, n_oc=16, wdma=dma_uq, nw=16, wbatch=1,
                            lhsT_view=lambda oc, kc: wsl[oc % 2][:, kc * 128:(kc + 1) * 128],
                            KC=KCE, rhs_view=rhs_cqc, NTC=1,
                            evac=evac_qc, evac_eng="ac")
                        p.c["sq"] += 16 * 16

                        def qc_out(sy, hc0=hc0, acb_qc=acb_qc):
                            for oc in range(16):
                                p.w(sy, "sy", "ac", acb_qc + oc + 1)
                                sy.dma_start(
                                    out=qc_scr[oc * 128:(oc + 1) * 128, hc0:hc0 + 512],
                                    in_=qstg2[:, (oc % 2) * 512:(oc % 2) * 512 + 512]
                                    ).then_inc(sems["sq"], 16)
                        block.sync(qc_out)
                        qc_free = [[("sq", sqb_qc + 16 * 15)], [("sq", sqb_qc + 16 * 16)]]
                        p.mb()
                        def dma_qr2(gp, j):
                            gp.dma_start(out=wsl[j % 2].ap()[:, 0:2048].rearrange(
                                "q (k m) -> q k m", m=128), in_=wqr2[j]).then_inc(sems["gq"], 16)
                        qr_free = emit_qr_pairs(
                            p, wdma_pair=dma_qr2,
                            lhsT_view=lambda j, kc: wsl[j % 2][:, kc * 128:(kc + 1) * 128],
                            rhs_view=rhs_cqc, W=512, rp2=rp2, qstg=qstg[:, :],
                            qr_scr=qr_scr, c0=hc0,
                            sq_free=qr_free)
                        qr_free = list(qr_free)
                        p.mb()
                    def dma_dkv(gp, bi):
                        gp.dma_start(out=wsl[bi % 2].ap()[:, 0:2048].rearrange(
                            "q (k m) -> q k m", m=128), in_=wdkv[bi]).then_inc(sems["gq"], 16)
                    emit_proj(
                        p, n_oc=4, wdma=dma_dkv, nw=4, wbatch=1,
                        lhsT_view=lambda oc, kc: wsl[oc % 2][:, kc * 128:(kc + 1) * 128],
                        KC=KCE,
                        rhs_view=lambda kc, tc: xb[:, kc * 1024 + tc * 512: kc * 1024 + (tc + 1) * 512],
                        NTC=2,
                        evac=(lambda c0: lambda eng, oc, st_, ps: eng.activation(
                            ckv[:, oc * S + c0: oc * S + c0 + 1024], ps, AF.Copy))(c0),
                        evac_eng="ac")
                    p.mb()
                    kr_wslf = list(p.wsl_free)

                    def dma_kr(gp):
                        p.wl(gp, "gp", kr_wslf[0])
                        gp.dma_start(out=wsl[0].ap()[:, 0:16 * 64].rearrange(
                            "q (k m) -> q k m", m=64),
                            in_=wkr[0]).then_inc(sems["gq"], 16)
                    emit_rope_proj(
                        p, wdma=dma_kr,
                        lhsT_view=lambda kc: wsl[0][:, kc * 64:(kc + 1) * 64],
                        KC=KCE,
                        rhs_view=lambda kc, tc: xb[:, kc * 1024 + tc * 512: kc * 1024 + (tc + 1) * 512],
                        dest=kR, rp=rp, NTC=2, tok0=c0)
                    p.wsl_free = [[("pe", p.c["pe"])], [("pe", p.c["pe"])]]
                    p.mb()
                if True:
                    p.tap("ckv", ckv[:, :], [128, 4 * S], BF16)
                    p.tap("kr", kR[0:64, :], [64, S], BF16)
                    p.tap("qcs", qc_scr, [E, S], BF16)
                    p.tap("qrs", qr_scr, [1024, S], BF16)
                    p.ck("p2")

                with contextlib.ExitStack() as ph:
                    ten = {
                        "qch2": [ph.enter_context(sb(f"qch{i}", [128, S], BF16)) for i in range(2)],
                        "qrh2": [ph.enter_context(sb(f"qrh{i}", [64, S], BF16)) for i in range(2)],
                        "kh": ph.enter_context(sb("kh", [128, S], BF16)),
                        "vtm": ph.enter_context(sb("vtm", [128, 16 * 128], BF16)),
                        "probs": ph.enter_context(sb("probs", [128, 16 * 512], BF16)),
                        "wuk_sb": [ph.enter_context(sb(f"wuks{i}", [128, 4 * 128], BF16)) for i in range(2)],
                        "wuv_sb": [ph.enter_context(sb(f"wuvs{i}", [128, 4 * 128], BF16)) for i in range(2)],
                        "aosl": ph.enter_context(sb("aosl", [128, 2 * 512], BF16)),
                        "ckv": ckv[:, :], "kR": kR,
                        "qc_scr": qc_scr, "qr_scr": qr_scr,
                        "wuk": wuk, "wuvr": wuvr,
                        "ones_b": ones_b[:, :], "ones_fr": ones_fr[:, :],
                        "c_negB": c_negB[:, :], "c_zero": c_zero[0:1, 0:1],
                        "rs_row": rs_row, "rs_sb": rs_sb[:, :],
                        "maska_sb": maska_sb[:, :], "ao_scr": ao_scr,
                    }
                    hb = {"wfree": p.c["pe"], "qfree": p.c["pe"]}
                    for h in range(H):
                        emit_attn_a_head(p, h, hb, ten)
                        if h == 0 and _TAPS:
                            p.barrier()
                            p.tap("qch0", ten["qch2"][0][:, :], [128, S], BF16)
                            p.tap("qrh0", ten["qrh2"][0][0:64, :], [64, S], BF16)
                            p.tap("kh0", ten["kh"][:, :], [128, S], BF16)
                            p.tap("vtm0", ten["vtm"][:, :], [128, 16 * 128], BF16)
                        if h == 0:
                            p.ck("p3h0")
                        if h == 3:
                            p.ck("p3h3")
                    p.bs_owner = [[("ac", p.c["ac"]), ("dv", p.c["dv"])],
                                  [("ac", p.c["ac"]), ("dv", p.c["dv"])]]
                    p.wsl_free = [[("pe", p.c["pe"])], [("pe", p.c["pe"])]]
                    p3_done = {"pe": p.c["pe"], "dv": p.c["dv"], "sq": p.c["sq"]}
                    p.mb()
                p.tap("ao", ao_scr, [E, S], BF16)
                p.ck("p3")
                qstk.close()

                with contextlib.ExitStack() as ph:
                    ao_sb = ph.enter_context(sb("ao_sb", [128, KCE * S], BF16))
                    of32l = ph.enter_context(sb("of32a", [128, 2 * 1024], F32))
                    sqb0 = p.c["sq"]
                    p.c["sq"] += 16 * KCE

                    def ao_ld(sy):
                        p.w(sy, "sy", "pe", p3_done["pe"])
                        p.w(sy, "sy", "dv", p3_done["dv"])
                        p.w(sy, "sy", "sq", p3_done["sq"])
                        for kc in range(KCE):
                            sy.dma_start(out=ao_sb[:, kc * S:(kc + 1) * S],
                                         in_=ao_scr[kc * 128:(kc + 1) * 128, :]).then_inc(sems["sq"], 16)
                    block.sync(ao_ld)

                    def dma_wo(gp, bi):
                        gp.dma_start(out=wsl[bi % 2].ap()[:, 0:2048].rearrange(
                            "q (k m) -> q k m", m=128), in_=wo[bi]).then_inc(sems["gq"], 16)
                    emit_proj_accum(
                        p, n_oc=16, wdma=dma_wo, nw=16, wbatch=1,
                        lhsT_view=lambda oc, kc: wsl[oc % 2][:, kc * 128:(kc + 1) * 128],
                        KC=KCE,
                        rhs_view=lambda kc, tc: ao_sb[:, kc * S + tc * 512: kc * S + (tc + 1) * 512],
                        NTC=4,
                        dest_piece=lambda oc, pc: Xs[oc * 128:(oc + 1) * 128, pc * 1024:(pc + 1) * 1024],
                        of32=of32l[:, :],
                        pe_wait0=lambda pe: p.w(pe, "pe", "sq", sqb0 + 16 * KCE))
                    p.mb()
            p.tap("X1", Xs, [E, S], F32)
            p.ck("a2")

            ffn_app(sq, Xs, src_ready=[("gq", p.c["gq"])])
            p.tap("X2", Xs, [E, S], F32)
            p.ck("sl2")

            sl3_src_ready = [("gq", p.c["gq"])]
            for g in range(2):
                with contextlib.ExitStack() as gstk:
                    vtm_g = gstk.enter_context(sb("vtm_g", [128, 16 * 8 * 128], BF16))
                    Qg = gstk.enter_context(sb("Qg", [128, 8 * S], BF16))
                    Kg = gstk.enter_context(sb("Kg", [128, 8 * S], BF16))
                    for ch in range(2):
                      c0 = ch * 1024
                      with contextlib.ExitStack() as ph:
                        xb = ph.enter_context(sb("xb3", [128, KCE * 1024], BF16,
                                                 side="right"))
                        with contextlib.ExitStack() as lns:
                            t = alloc_ln(lns, nq=1, sc=128)
                            emit_ln(p, t, Xs,
                                    lambda kc, cc, w: xb[:, kc * 1024 + cc: kc * 1024 + cc + w],
                                    lambda cc, w: xb.ap().rearrange("q (kc n) -> q kc n", n=1024)[:, :, cc:cc + w],
                                    c0, 1024, SC=128,
                                    src_ready=(sl3_src_ready
                                               if sl3_src_ready is not None
                                               else [("pe", p.c["pe"]),
                                                     ("gq", p.c["gq"]),
                                                     ("dv", p.c["dv"])]),
                                    buf_ready=[("pe", p.c["pe"]),
                                               ("sq", p.c["sq"])])
                            sl3_src_ready = None
                            ln_dv = p.ln_prev["dv"]
                            p.mb()
                        with contextlib.ExitStack() as ws:
                            wvh = ws.enter_context(sb("wvh", [128, 16 * 512], BF16))
                            prev_pe = p.c["pe"]
                            bso_v = [list(p.bs_owner[0] or []), list(p.bs_owner[1] or [])]
                            for half in range(2):
                                gqb0 = p.c["gq"]
                                p.c["gq"] += 16
                                peb0, acb0 = p.c["pe"], p.c["ac"]
                                p.c["pe"] += 8
                                p.c["ac"] += 8

                                def wv_ld(gp, half=half, gqb0=gqb0, prev_pe=prev_pe):
                                    p.w(gp, "gp", "pe", prev_pe)
                                    p.w(gp, "gp", "dv", ln_dv)
                                    gp.dma_start(
                                        out=wvh.ap().rearrange("q (k m) -> q k m", m=512),
                                        in_=wvr[g].rearrange("k q m -> q k m")[:, :, half * 512:(half + 1) * 512]
                                    ).then_inc(sems["gq"], 16)
                                block.gpsimd(wv_ld)

                                def pe_f(pe, ch=ch, half=half, gqb0=gqb0, peb0=peb0, acb0=acb0):
                                    p.w(pe, "pe", "gq", gqb0 + 16)
                                    p.w(pe, "pe", "ac", acb0)
                                    if half == 0:
                                        p.wl(pe, "pe", bso_v[0])
                                        p.wl(pe, "pe", bso_v[1])
                                        p.w(pe, "pe", "dv", ln_dv)
                                    for mm in range(8):
                                        m = 8 * ch + mm
                                        ins = None
                                        for kc in range(KCE):
                                            ins = pe.matmul(
                                                bk(p.PS, mm),
                                                xb[:, kc * 1024 + mm * 128: kc * 1024 + (mm + 1) * 128],
                                                wvh[:, kc * 512:(kc + 1) * 512],
                                                start=(kc == 0), stop=(kc == KCE - 1))
                                        ins.then_inc(p.sems["pe"], 1)
                                block.tensor(pe_f)

                                def ac_f(ac, ch=ch, half=half, peb0=peb0):
                                    for mm in range(8):
                                        m = 8 * ch + mm
                                        p.w(ac, "ac", "pe", peb0 + mm + 1)
                                        ac.activation(
                                            vtm_g[:, m * 1024 + half * 512: m * 1024 + half * 512 + 512],
                                            bk(p.PS, mm),
                                            AF.Copy).then_inc(p.sems["ac"], 1)
                                block.scalar(ac_f)
                                prev_pe = peb0 + 8
                            p.bs_owner = [[("ac", p.c["ac"])], [("ac", p.c["ac"])]]
                            p.mb()

                        for wsrc, dst in [(wq, Qg), (wk, Kg)]:
                            def dma_w(gp, bi, wsrc=wsrc):
                                gp.dma_start(out=wsl[bi % 2].ap()[:, 0:2048].rearrange(
                                    "q (k m) -> q k m", m=128),
                                    in_=wsrc[g * 8 + bi]).then_inc(sems["gq"], 16)
                            emit_proj(
                                p, n_oc=8, wdma=dma_w, nw=8, wbatch=1,
                                lhsT_view=lambda oc, kc: wsl[oc % 2][:, kc * 128:(kc + 1) * 128],
                                KC=KCE,
                                rhs_view=lambda kc, tc: xb[:, kc * 1024 + tc * 512: kc * 1024 + (tc + 1) * 512],
                                NTC=2,
                                evac=(lambda dst, c0: lambda eng, oc, st_, ps: eng.activation(
                                    dst[:, oc * S + c0: oc * S + c0 + 1024], ps, AF.Copy))(dst, c0),
                                evac_eng="ac")
                            p.mb()

                    ao_g = gstk.enter_context(sb("ao_g", [128, 8 * S], BF16,
                                                 side="right"))
                    probs3 = gstk.enter_context(sb("probs3", [128, 25 * 512], BF16))
                    ten3 = {
                        "Qg": Qg[:, :], "Kg": Kg[:, :], "vtm_g": vtm_g[:, :],
                        "ao_g": ao_g[:, :], "probs": probs3[:, :],
                        "ones_b": ones_b[:, :], "ones_fr": ones_fr[:, :],
                        "c_negB": c_negB[:, :], "c_zero": c_zero[0:1, 0:1],
                        "rs_row": rs_row, "rs_sb": rs_sb[:, :],
                        "maskw_sb": maskw_sb[:, :],
                    }
                    hb3 = {"first": True,
                           "attnv3": p.c["pe"],
                           "sumsln": [p.c["ac"], p.c["ac"]],
                           "aoev": [p.c["dv"], p.c["dv"]],
                           "rscopy": [p.c["ac"], p.c["ac"]]}
                    for hl in range(8):
                        emit_attn_sw(p, hl, hb3, ten3)
                    p.bs_owner = [[("ac", p.c["ac"]), ("dv", p.c["dv"])],
                                  [("ac", p.c["ac"]), ("dv", p.c["dv"])]]
                    attnb_dv = p.c["dv"]
                    p.mb()
                    if g == 0:
                        p.tap("aog0", ao_g[:, :], [128, 8 * S], BF16)
                    else:
                        p.tap("aog1", ao_g[:, :], [128, 8 * S], BF16)
                    p.c["sq"] += 16 * 8

                    def ao_dump(sy, g=g, ao_ap=ao_g[:, :], attnb_dv=attnb_dv):
                        p.w(sy, "sy", "dv", attnb_dv)
                        for hl in range(8):
                            sy.dma_start(
                                out=ao_scr[(g * 8 + hl) * 128:(g * 8 + hl + 1) * 128, :],
                                in_=ao_ap[:, hl * S:(hl + 1) * S]).then_inc(sems["sq"], 16)
                    block.sync(ao_dump)
                    p.mb()
                    p.ck(f"sl3g{g}")

            with contextlib.ExitStack() as ph:
                ao_sb = ph.enter_context(sb("ao_sb3", [128, KCE * S], BF16))
                of32w = ph.enter_context(sb("of32w", [128, 2 * 1024], F32))
                sqb0 = p.c["sq"]
                sl3_done = {"pe": p.c["pe"], "dv": p.c["dv"]}
                p.c["sq"] += 16 * KCE

                def ao_ld(sy):
                    p.w(sy, "sy", "pe", sl3_done["pe"])
                    p.w(sy, "sy", "dv", sl3_done["dv"])
                    p.w(sy, "sy", "sq", sqb0)
                    for kc in range(KCE):
                        sy.dma_start(out=ao_sb[:, kc * S:(kc + 1) * S],
                                     in_=ao_scr[kc * 128:(kc + 1) * 128, :]).then_inc(sems["sq"], 16)
                block.sync(ao_ld)

                def dma_woo(gp, bi):
                    gp.dma_start(out=wsl[bi % 2].ap()[:, 0:2048].rearrange(
                        "q (k m) -> q k m", m=128), in_=woo[bi]).then_inc(sems["gq"], 16)
                emit_proj_accum(
                    p, n_oc=16, wdma=dma_woo, nw=16, wbatch=1,
                    lhsT_view=lambda oc, kc: wsl[oc % 2][:, kc * 128:(kc + 1) * 128],
                    KC=KCE,
                    rhs_view=lambda kc, tc: ao_sb[:, kc * S + tc * 512: kc * S + (tc + 1) * 512],
                    NTC=4,
                    dest_piece=lambda oc, pc: Xs[oc * 128:(oc + 1) * 128, pc * 1024:(pc + 1) * 1024],
                    of32=of32w[:, :],
                    pe_wait0=lambda pe: p.w(pe, "pe", "sq", sqb0 + 16 * KCE))
                p.mb()
            p.tap("X3", Xs, [E, S], F32)
            p.ck("sl3")

            p.c["gq"] += 16

            def ycopy(gp):
                p.w(gp, "gp", "gq", p.c["gq"] - 16)
                gp.dma_start(out=y_out[sq], in_=Xs).then_inc(sems["gq"], 16)
            block.gpsimd(ycopy)
            p.mb()
            ffn_app(sq, y_out[sq], src_ready=[("gq", p.c["gq"])])
            p.ck("sl4")

        except _StopEmission:
            pass
        if "fdump" in _TAPS:
            xd = nc.dram_tensor("tap_fdx", [NSEQ, E, S], F32,
                                kind="ExternalOutput").ap()
            ad = nc.dram_tensor("tap_fda", [E, S], BF16,
                                kind="ExternalOutput").ap()
            p.tap_names += ["fdx", "fda"]
            fsnap = dict(p.c)
            p.c["sq"] += 16 * (NSEQ + 1)

            def fdump(sy):
                for s_, v in fsnap.items():
                    p.w(sy, "sy", s_, v)
                for q in range(NSEQ):
                    sy.dma_start(out=xd[q], in_=X_int[q]).then_inc(sems["sq"], 16)
                sy.dma_start(out=ad, in_=ao_scr).then_inc(sems["sq"], 16)
            block.sync(fdump)
        snap = dict(p.c)

        def fin(sy):
            for s, v in snap.items():
                p.w(sy, "sy", s, v)
        block.sync(fin)
        block.__exit__(None, None, None)
        nc._top_stack = top

    return nc, p


def kernel(**inp):
    nc, p = build_nc()
    kernel.mb_sites = getattr(p, "mb_sites", [])
    arrs = prepare_weights(inp)
    x = inp["L0_attn_a_x"]
    in_maps = []
    for c in range(8):
        m = dict(arrs)
        m["x"] = np.ascontiguousarray(
            x[2 * c:2 * c + 2].transpose(0, 2, 1)).astype(np.float32, copy=False)
        in_maps.append(m)
    _tr = bool(int(_os.environ.get("ATHENA_TRACE", "0")))
    res = run_bass_kernel_spmd(nc, in_maps, core_ids=list(range(8)),
                               trace=_tr, trace_cores=[0] if _tr else None)
    out = np.empty((16, 2048, 2048), np.float32)
    for c in range(8):
        y = np.asarray(res.results[c]["y"])
        out[2 * c:2 * c + 2] = y.transpose(0, 2, 1)
    if _TAPS:
        kernel.taps = [{n: np.asarray(res.results[c]["tap_" + n]) for n in p.tap_names}
                       for c in range(8)]
    return out
